# version 02
# baseline (speedup 1.0000x reference)
import numpy as np
import concourse.bass as bass
import concourse.mybir as mybir
from concourse.bass_utils import run_bass_kernel_spmd
from contextlib import ExitStack

F32 = mybir.dt.float32
BF16 = mybir.dt.bfloat16
AF = mybir.ActivationFunctionType
ALU = mybir.AluOpType
AX = mybir.AxisListType

TOK = 4096
NB = 8
TB = 512
D = 1024
DC = 8
FF = 4096
FC = 32
NPAIR = 4
EPS = 1e-5
NCH = TOK // 128
NK = 256

DBG_PHASES = "ABCD"
DBG_DUMP = False
DBG_ZERO_R = False


def build_program():
    nc = bass.Bass("TRN2", target_bir_lowering=False)

    x_d = nc.dram_tensor("x", [TOK, D], F32, kind="ExternalInput").ap()
    wq_d = nc.dram_tensor("wq", [D, D], F32, kind="ExternalInput").ap()
    wk_d = nc.dram_tensor("wk", [D, D], F32, kind="ExternalInput").ap()
    wv_d = nc.dram_tensor("wv", [D, D], F32, kind="ExternalInput").ap()
    wo_d = nc.dram_tensor("wo", [D, D], F32, kind="ExternalInput").ap()
    bq_d = nc.dram_tensor("bq", [128, 8], F32, kind="ExternalInput").ap()
    bk_d = nc.dram_tensor("bk", [128, 8], F32, kind="ExternalInput").ap()
    bv_d = nc.dram_tensor("bv", [128, 8], F32, kind="ExternalInput").ap()
    bo_d = nc.dram_tensor("bo", [128, 8], F32, kind="ExternalInput").ap()
    wit_d = nc.dram_tensor("wit", [D, FF], F32, kind="ExternalInput").ap()
    wib_d = nc.dram_tensor("wib", [128, 32], F32, kind="ExternalInput").ap()
    wot_d = nc.dram_tensor("wot", [FF, D], F32, kind="ExternalInput").ap()
    wob_d = nc.dram_tensor("wob", [128, 8], F32, kind="ExternalInput").ap()
    gwt_d = nc.dram_tensor("gwt", [D, 8], F32, kind="ExternalInput").ap()
    gb_d = nc.dram_tensor("gb", [8, 1], F32, kind="ExternalInput").ap()
    acatt_d = nc.dram_tensor("acatt", [D, 32], F32, kind="ExternalInput").ap()
    bcat_d = nc.dram_tensor("bcat", [32, FF], F32, kind="ExternalInput").ap()
    ident_d = nc.dram_tensor("ident", [128, 128], F32, kind="ExternalInput").ap()
    ones16_d = nc.dram_tensor("ones16", [128, 16], F32, kind="ExternalInput").ap()
    selr_d = nc.dram_tensor("selr", [16, 1024], F32, kind="ExternalInput").ap()
    selm_d = nc.dram_tensor("selm", [8, 32], F32, kind="ExternalInput").ap()
    epsc_d = nc.dram_tensor("epsc", [128, 2], F32, kind="ExternalInput").ap()
    y_d = nc.dram_tensor("y", [TOK, D], F32, kind="ExternalOutput").ap()
    if DBG_DUMP:
        qdump_d = nc.dram_tensor("qdump", [128, 4, 8192], BF16,
                                 kind="ExternalOutput").ap()
        kdump_d = nc.dram_tensor("kdump", [128, 4, 8192], BF16,
                                 kind="ExternalOutput").ap()
        vdump_d = nc.dram_tensor("vdump", [DC, 128, TOK], BF16,
                                 kind="ExternalOutput").ap()
        rdump_d = nc.dram_tensor("rdump", [32, 2048], BF16,
                                 kind="ExternalOutput").ap()
        edump_d = nc.dram_tensor("edump", [128, 2, 8192], BF16,
                                 kind="ExternalOutput").ap()
        cadump_d = nc.dram_tensor("cadump", [128, 2, 8], F32,
                                  kind="ExternalOutput").ap()
        lhsdump_d = nc.dram_tensor("lhsdump", [128, 2, 1536], BF16,
                                   kind="ExternalOutput").ap()
        x1dump_d = nc.dram_tensor("x1dump", [TOK, D], F32,
                                  kind="ExternalOutput").ap()
        h2dump_d = nc.dram_tensor("h2dump", [DC, 128, TOK], BF16,
                                  kind="ExternalOutput").ap()
    vsp_d = nc.dram_tensor("vsp", [DC, 128, TOK], BF16).ap()
    x1_d = nc.dram_tensor("x1buf", [TOK, D], F32).ap()
    h2t_d = nc.dram_tensor("h2tbuf", [DC, 128, TOK], BF16).ap()

    with ExitStack() as top:
        e_ = top.enter_context
        bq = e_(nc.sbuf_tensor("bq_s", [128, 8], F32))
        bk = e_(nc.sbuf_tensor("bk_s", [128, 8], F32))
        bv = e_(nc.sbuf_tensor("bv_s", [128, 8], F32))
        bo = e_(nc.sbuf_tensor("bo_s", [128, 8], F32))
        wib = e_(nc.sbuf_tensor("wib_s", [128, 32], F32))
        wob = e_(nc.sbuf_tensor("wob_s", [128, 8], F32))
        gb = e_(nc.sbuf_tensor("gb_s", [8, 1], F32))
        identb = e_(nc.sbuf_tensor("identb", [128, 128], BF16))
        identf8 = e_(nc.sbuf_tensor("identf8", [8, 8], F32))
        ones16 = e_(nc.sbuf_tensor("ones16_s", [128, 16], BF16))
        selm = e_(nc.sbuf_tensor("selm_s", [8, 32], BF16))
        gwt = e_(nc.sbuf_tensor("gwt_s", [128, DC, 8], BF16))
        acatt = e_(nc.sbuf_tensor("acatt_s", [128, DC, 32], BF16))
        epsb = e_(nc.sbuf_tensor("epsb", [128, 2], F32))
        r_ev = e_(nc.sbuf_tensor("r_ev", [16, 2048], BF16))
        r_od = e_(nc.sbuf_tensor("r_od", [16, 2048], BF16))

        sems = {}
        cnt = {}

        def sem(name):
            if name not in sems:
                sems[name] = e_(nc.semaphore(name))
                cnt[name] = 0
            return sems[name]

        def W(eng, name, v):
            if v > 0:
                eng.wait_ge(sem(name), v)

        def I(instr, name, k=1):
            instr.then_inc(sem(name), k)
            cnt[name] += k
            return instr

        for _n in ["sA", "sG", "s_dv", "s_ac", "s_h", "s_var", "s_rstd",
                   "s_trev", "s_mmtr", "s_mmq", "s_qev", "s_vact",
                   "s_xfree", "s_qkvdone", "s_fill", "s_exp", "s_lhs",
                   "s_rmm", "s_rev",
                   "s3_rb", "s3_rvc", "s3_wo", "s3_ev", "s3_at", "s3_ae",
                   "s3_var", "s3_rstd", "s3_x1", "s3_h2", "s3_h2a",
                   "s3_ath2",
                   "s4_gmm", "s4_lg", "s4_lgtmm", "s4_lgt", "s4_mk",
                   "s4_mtr", "s4_mkt", "s4_m32mm", "s4_xre", "s4_xrmm",
                   "s4_mx", "s4_wimm", "s4_ie", "s4_womm", "s4_fe",
                   "s4_ftr", "s4_ffe", "s4_var", "s4_rstd", "s4_y", "s_eps",
                   "sX0", "sX1", "sVsp0", "sVsp1", "sG3",
                   "sVld0", "sVld1", "sX30", "sX31", "sX1o0", "sX1o1",
                   "sH2o0", "sH2o1", "sH2i0", "sH2i1", "sX1i0", "sX1i1",
                   "sYo0", "sYo1", "sWot0", "sWot1"]:
            sem(_n)

        with ExitStack() as scqt:
            eq_ = scqt.enter_context
            qt = eq_(nc.sbuf_tensor("qt", [128, NPAIR, 8192], BF16))
            kt = eq_(nc.sbuf_tensor("kt", [128, NPAIR, 8192], BF16))

            with ExitStack() as sc1:
                e1 = sc1.enter_context
                wq = e1(nc.sbuf_tensor("wq_s", [128, DC, D], BF16))
                wk = e1(nc.sbuf_tensor("wk_s", [128, DC, D], BF16))
                wv = e1(nc.sbuf_tensor("wv_s", [128, DC, D], BF16))
                xb = e1(nc.sbuf_tensor("xb", [128, 2, D], F32))
                hbf = e1(nc.sbuf_tensor("hbf", [128, 2, D], BF16))
                hT = e1(nc.sbuf_tensor("hT", [128, DC, TB], BF16))
                vt_tmp = e1(nc.sbuf_tensor("vt_tmp", [128, 2, TB], BF16))
                st_xs = e1(nc.sbuf_tensor("st_xs", [128, 1], F32))
                st_xq = e1(nc.sbuf_tensor("st_xq", [128, 1], F32))
                st_nm = e1(nc.sbuf_tensor("st_nm", [128, 1], F32))
                st_n2 = e1(nc.sbuf_tensor("st_n2", [128, 1], F32))
                st_vr = e1(nc.sbuf_tensor("st_vr", [128, 1], F32))
                st_lv = e1(nc.sbuf_tensor("st_lv", [128, 1], F32))
                st_rs = e1(nc.sbuf_tensor("st_rs", [128, 1], F32))
                ps_tr = e1(nc.psum_tensor("ps_tr", [128, 2, 1024], BF16))
                ps_qk = e1(nc.psum_tensor("ps_qk", [128, 2, 512], F32))
                blockA = e1(nc.Block())

                N_PRE_A = 9
                N_GPRE = 8

                @blockA.sync
                def _(sync):
                    for dst, src in [
                        (bq[:, :], bq_d), (bk[:, :], bk_d), (bv[:, :], bv_d),
                        (bo[:, :], bo_d), (wib[:, :], wib_d),
                        (wob[:, :], wob_d), (gb[:, :], gb_d),
                        (identf8[:, :], ident_d[0:8, 0:8]),
                        (epsb[:, :], epsc_d),
                    ]:
                        I(sync.dma_start(dst, src), "sA", 16)
                    if "A" not in DBG_PHASES:
                        sync.wait_ge(sem("sA"), cnt["sA"])
                        return
                    for gi in range(NCH):
                        W(sync, "s_xfree", gi - 1)
                        I(sync.dma_start(xb[:, gi % 2, :],
                                         x_d[gi * 128:(gi + 1) * 128, :]),
                          "sX%d" % (gi % 2), 16)

                @blockA.gpsimd
                def _(gpsimd):
                    loads = [
                        (wq[:, :, :],
                         wq_d.rearrange("(ci p) f -> p ci f", p=128)),
                        (wk[:, :, :],
                         wk_d.rearrange("(ci p) f -> p ci f", p=128)),
                        (wv[:, :, :],
                         wv_d.rearrange("(ci p) f -> p ci f", p=128)),
                        (identb[:, :], ident_d),
                        (ones16[:, :], ones16_d),
                        (selm[:, :], selm_d),
                        (gwt[:, :, :],
                         gwt_d.rearrange("(ci p) f -> p ci f", p=128)),
                        (acatt[:, :, :],
                         acatt_d.rearrange("(ci p) f -> p ci f", p=128)),
                    ]
                    assert len(loads) == N_GPRE
                    for dst, src in loads:
                        I(gpsimd.dma_start(dst, src), "sG", 16)
                    if "A" not in DBG_PHASES:
                        gpsimd.wait_ge(sem("sG"), cnt["sG"])
                        return
                    for e in range(NB):
                        for co in range(DC):
                            vidx = e * DC + co
                            W(gpsimd, "s_vact", vidx + 1)
                            I(gpsimd.dma_start(
                                vsp_d[co, :, e * TB:(e + 1) * TB],
                                vt_tmp[:, vidx % 2, :]),
                              "sVsp%d" % (vidx % 2), 16)
                    gpsimd.wait_ge(sem("sG"), cnt["sG"])
                    gpsimd.wait_ge(sem("sVsp0"), cnt["sVsp0"])
                    gpsimd.wait_ge(sem("sVsp1"), cnt["sVsp1"])

                @blockA.vector
                def _(vector):
                    if "A" not in DBG_PHASES:
                        return
                    for gi in range(NCH):
                        e = gi // 4
                        cc4 = gi % 4
                        W(vector, "sX%d" % (gi % 2), 16 * (gi // 2 + 1))
                        xv = xb[:, gi % 2, :]
                        W(vector, "s_mmtr", gi - 1)
                        v0 = cnt["s_dv"]
                        I(vector.reduce_sum(st_xs[:, 0:1], xv, axis=AX.X),
                          "s_dv")
                        I(vector.scalar_tensor_tensor(
                            hbf[:, gi % 2, :], xv, 1.0, xv, ALU.mult,
                            ALU.mult, accum_out=st_xq[:, 0:1]), "s_dv")
                        vector.wait_ge(sem("s_dv"), v0 + 2)
                        I(vector.tensor_scalar(st_nm[:, 0:1], st_xs[:, 0:1],
                                               -1.0 / D, None, ALU.mult),
                          "s_dv")
                        vector.wait_ge(sem("s_dv"), v0 + 3)
                        I(vector.tensor_mul(st_n2[:, 0:1], st_nm[:, 0:1],
                                            st_nm[:, 0:1]), "s_dv")
                        vector.wait_ge(sem("s_dv"), v0 + 4)
                        I(vector.scalar_tensor_tensor(
                            st_vr[:, 0:1], st_xq[:, 0:1], 1.0 / D,
                            st_n2[:, 0:1], ALU.mult, ALU.subtract), "s_var")
                        W(vector, "s_rstd", gi + 1)
                        I(vector.tensor_scalar(hbf[:, gi % 2, :], xv,
                                               st_nm[:, 0:1], st_rs[:, 0:1],
                                               ALU.add, ALU.mult), "s_h")
                        I(vector.engine_nop(), "s_xfree")
                        W(vector, "s_mmtr", gi + 1)
                        W(vector, "s_qkvdone", e)
                        I(vector.tensor_copy(
                            hT[:, :, cc4 * 128:(cc4 + 1) * 128],
                            ps_tr[:, gi % 2, :]
                            .rearrange("p (c t) -> p c t", t=128)), "s_trev")
                        if gi % 4 == 3:
                            p4 = e // 2
                            rh = (e % 2) * 64
                            for pi, (buf, bias) in enumerate(
                                    [(qt, bq), (kt, bk), (None, bv)]):
                                for co in range(DC):
                                    idx = e * 24 + pi * 8 + co
                                    W(vector, "s_mmq", idx + 1)
                                    pq = ps_qk[:, idx % 2, :]
                                    if pi < 2:
                                        I(vector.tensor_scalar(
                                            buf[rh:rh + 64, p4,
                                                2 * co * TB:
                                                (2 * co + 1) * TB],
                                            pq[0:64, :],
                                            bias[0:64, co:co + 1], None,
                                            ALU.add), "s_dv")
                                        I(vector.tensor_scalar(
                                            buf[rh:rh + 64, p4,
                                                (2 * co + 1) * TB:
                                                (2 * co + 2) * TB],
                                            pq[64:128, :],
                                            bias[64:128, co:co + 1], None,
                                            ALU.add), "s_qev")
                                    else:
                                        vidx = e * DC + co
                                        W(vector, "sVsp%d" % (vidx % 2),
                                          16 * (vidx // 2))
                                        I(vector.tensor_scalar(
                                            vt_tmp[:, vidx % 2, :],
                                            pq[:, :],
                                            bias[:, co:co + 1], None,
                                            ALU.add), "s_qev")
                                        I(vector.engine_nop(), "s_vact")

                @blockA.scalar
                def _(scalar):
                    if "A" not in DBG_PHASES:
                        return
                    W(scalar, "sA", 16 * N_PRE_A)
                    for gi in range(NCH):
                        W(scalar, "s_var", gi + 1)
                        W(scalar, "s_h", gi)
                        a0 = cnt["s_ac"]
                        I(scalar.activation(st_lv[:, 0:1], st_vr[:, 0:1],
                                            AF.Ln, bias=epsb[:, 0:1]), "s_ac")
                        scalar.wait_ge(sem("s_ac"), a0 + 1)
                        I(scalar.activation(st_rs[:, 0:1], st_lv[:, 0:1],
                                            AF.Exp, scale=-0.5), "s_rstd")

                @blockA.tensor
                def _(tensor):
                    if "A" not in DBG_PHASES:
                        return
                    W(tensor, "sG", 16 * N_GPRE)
                    for gi in range(NCH):
                        e = gi // 4
                        W(tensor, "s_h", gi + 1)
                        W(tensor, "s_trev", gi - 1)
                        for j in range(DC):
                            mm = tensor.matmul(
                                ps_tr[:, gi % 2, j * 128:(j + 1) * 128],
                                hbf[:, gi % 2, j * 128:(j + 1) * 128],
                                identb[:, :], is_transpose=True,
                                start=(j == 0), stop=(j == DC - 1))
                        I(mm, "s_mmtr")
                        if gi % 4 == 3:
                            W(tensor, "s_trev", 4 * (e + 1))
                            for pi, wmat in enumerate([wq, wk, wv]):
                                for co in range(DC):
                                    idx = e * 24 + pi * 8 + co
                                    W(tensor, "s_qev", idx - 1)
                                    pq = ps_qk[:, idx % 2, :]
                                    for ci in range(DC):
                                        mm = tensor.matmul(
                                            pq,
                                            wmat[:, ci,
                                                 co * 128:(co + 1) * 128],
                                            hT[:, ci, :],
                                            start=(ci == 0),
                                            stop=(ci == DC - 1))
                                    I(mm, "s_mmq")
                            I(tensor.nop(), "s_qkvdone")

            if DBG_DUMP:
                with nc.Block() as blkdbg1:
                    @blkdbg1.gpsimd
                    def _(gpsimd):
                        if "A" in DBG_PHASES:
                            I(gpsimd.dma_start(qdump_d, qt[:, :, :]),
                              "sG", 16)
                            I(gpsimd.dma_start(kdump_d, kt[:, :, :]),
                              "sG", 16)
                            I(gpsimd.dma_start(vdump_d, vsp_d), "sG", 16)
                        gpsimd.wait_ge(sem("sG"), cnt["sG"])

            with ExitStack() as sc2:
                e2 = sc2.enter_context
                eA = e2(nc.sbuf_tensor("eA", [128, 2, 8192], BF16))
                eB = e2(nc.sbuf_tensor("eB", [128, 2, 8192], BF16))
                lhsTall = e2(nc.sbuf_tensor("lhsTall", [128, 2, 1536], BF16))
                caA = e2(nc.sbuf_tensor("caA", [128, 2, 8], F32))
                caB = e2(nc.sbuf_tensor("caB", [128, 2, 8], F32))
                cA = e2(nc.sbuf_tensor("cA", [128, 1], F32))
                cB = e2(nc.sbuf_tensor("cB", [128, 1], F32))
                icA = e2(nc.sbuf_tensor("icA", [128, 1], F32))
                icB = e2(nc.sbuf_tensor("icB", [128, 1], F32))
                icAb = e2(nc.sbuf_tensor("icAb", [128, 1], BF16))
                icBb = e2(nc.sbuf_tensor("icBb", [128, 1], BF16))
                icAr = e2(nc.sbuf_tensor("icAr", [128, 1], F32))
                icBr = e2(nc.sbuf_tensor("icBr", [128, 1], F32))
                ps_sl = e2(nc.psum_tensor("ps_sl", [128, 3, 1024], F32))
                rps = e2(nc.psum_tensor("rps", [48, 512], F32))
                blockB = e2(nc.Block())

                def lhsT_ap(par, j, head):
                    off = par * 1536 + j * 96 + head * 48
                    return bass.AP(lhsTall, off, [[2 * 1536, 128], [1, 48]])

                def lhsT_write_ap(par, head):
                    off = par * 1536 + head * 80
                    return bass.AP(lhsTall, off, [[2 * 1536, 128], [97, 16]])

                @blockB.tensor
                def _(tensor):
                    if "B" not in DBG_PHASES:
                        return
                    def emit_R(Kr):
                        if Kr < 0:
                            return
                        p, k = Kr // 64, Kr % 64
                        par = Kr % 2
                        W(tensor, "s_exp", (Kr + 1) * 16)
                        W(tensor, "s_lhs", Kr + 1)
                        if k == 0:
                            W(tensor, "s_rev", p)
                        for j in range(16):
                            for head in range(2):
                                ebuf = eA if head == 0 else eB
                                mm = tensor.matmul(
                                    rps[:, :], lhsT_ap(par, j, head),
                                    ebuf[:, par, j * 512:(j + 1) * 512],
                                    start=(k == 0 and j == 0 and head == 0),
                                    stop=(k == 63 and j == 15 and head == 1))
                        I(mm, "s_rmm")

                    for K in range(NK):
                        p, k = K // 64, K % 64
                        for s in range(16):
                            gs = K * 16 + s
                            head = s % 2
                            s2 = s // 2
                            sl = gs % 3
                            rb = head * 64
                            W(tensor, "s_exp", gs - 2)
                            lhsT = kt[rb:rb + 64, p, k * 128:(k + 1) * 128]
                            for half in range(2):
                                a0 = s2 * 1024 + half * 512
                                mm = tensor.matmul(
                                    ps_sl[:, sl,
                                          half * 512:(half + 1) * 512],
                                    lhsT,
                                    qt[rb:rb + 64, p, a0:a0 + 512],
                                    start=True, stop=True,
                                    tile_position=(rb, 0))
                            I(mm, "s_fill")
                        emit_R(K - 1)
                    emit_R(NK - 1)

                @blockB.scalar
                def _(scalar):
                    if "B" not in DBG_PHASES:
                        return
                    for K in range(NK):
                        par = K % 2
                        W(scalar, "s_rmm", K - 1)
                        W(scalar, "s_lhs", K - 1)
                        for s in range(16):
                            gs = K * 16 + s
                            head = s % 2
                            s2 = s // 2
                            sl = gs % 3
                            W(scalar, "s_fill", gs + 1)
                            ebuf = eA if head == 0 else eB
                            cab = caA if head == 0 else caB
                            I(scalar.activation(
                                ebuf[:, par, s2 * 1024:(s2 + 1) * 1024],
                                ps_sl[:, sl, :], AF.Exp,
                                scale=float(2.0 ** -10),
                                bias=epsb[:, 1:2],
                                accum_out=cab[:, par, s2:s2 + 1]), "s_exp")

                @blockB.vector
                def _(vector):
                    if "B" not in DBG_PHASES:
                        return
                    for K in range(NK):
                        par = K % 2
                        W(vector, "s_exp", (K + 1) * 16)
                        W(vector, "s_rmm", K - 1)
                        v0 = cnt["s_dv"]
                        I(vector.reduce_sum(cA[:, 0:1], caA[:, par, :],
                                            axis=AX.X), "s_dv")
                        I(vector.reduce_sum(cB[:, 0:1], caB[:, par, :],
                                            axis=AX.X), "s_dv")
                        vector.wait_ge(sem("s_dv"), v0 + 2)
                        I(vector.reciprocal(icA[:, 0:1], cA[:, 0:1]), "s_dv")
                        I(vector.reciprocal(icB[:, 0:1], cB[:, 0:1]), "s_dv")
                        I(vector.memset(lhsTall[:, par, :], 0.0), "s_dv")
                        vector.wait_ge(sem("s_dv"), v0 + 5)
                        I(vector.tensor_scalar(icAb[:, 0:1],
                                               ones16[:, 0:1], icA[:, 0:1],
                                               None, ALU.mult), "s_dv")
                        I(vector.tensor_scalar(icBb[:, 0:1],
                                               ones16[:, 0:1], icB[:, 0:1],
                                               None, ALU.mult), "s_dv")
                        vector.wait_ge(sem("s_dv"), v0 + 7)
                        I(vector.tensor_copy(icAr[:, 0:1], icAb[:, 0:1]),
                          "s_dv")
                        I(vector.tensor_copy(icBr[:, 0:1], icBb[:, 0:1]),
                          "s_dv")
                        vector.wait_ge(sem("s_dv"), v0 + 9)
                        I(vector.tensor_scalar(lhsT_write_ap(par, 0),
                                               ones16[:, :], icAr[:, 0:1],
                                               None, ALU.mult), "s_dv")
                        I(vector.tensor_scalar(lhsT_write_ap(par, 1),
                                               ones16[:, :], icBr[:, 0:1],
                                               None, ALU.mult), "s_lhs")
                        if K % 64 == 63:
                            pp = K // 64
                            W(vector, "s_rmm", K + 1)
                            I(vector.tensor_scalar(
                                r_ev[0:16, pp * 512:(pp + 1) * 512],
                                rps[0:16, :], -1.0, None, ALU.add), "s_dv")
                            I(vector.tensor_scalar(
                                r_od[0:16, pp * 512:(pp + 1) * 512],
                                rps[32:48, :], -1.0, None, ALU.add),
                              "s_rev")

        if DBG_DUMP:
            with nc.Block() as blkdbg2:
                @blkdbg2.gpsimd
                def _(gpsimd):
                    if "B" in DBG_PHASES:
                        I(gpsimd.dma_start(rdump_d[0:16, :], r_ev[:, :]),
                          "sG", 16)
                        I(gpsimd.dma_start(rdump_d[16:32, :], r_od[:, :]),
                          "sG", 16)
                        I(gpsimd.dma_start(edump_d, eA[:, :, :]), "sG", 16)
                        I(gpsimd.dma_start(cadump_d, caA[:, :, :]),
                          "sG", 16)
                        I(gpsimd.dma_start(lhsdump_d, lhsTall[:, :, :]),
                          "sG", 16)
                    gpsimd.wait_ge(sem("sG"), cnt["sG"])

        with ExitStack() as sc34:
            e34 = sc34.enter_context
            wit = e34(nc.sbuf_tensor("wit_s", [128, DC, FF], BF16))
            selr = e34(nc.sbuf_tensor("selr_s", [16, 1024], BF16))
            bcat = e34(nc.sbuf_tensor("bcat_s", [32, FF], BF16))
            baseG_C = [0]

            with ExitStack() as sc3:
                e3 = sc3.enter_context
                wo_s = e3(nc.sbuf_tensor("wo_s", [128, DC, D], BF16))
                vt3 = e3(nc.sbuf_tensor("vt3", [128, 2, DC, TB], BF16))
                rv = e3(nc.sbuf_tensor("rv", [128, 2, DC, TB], BF16))
                at_sb = e3(nc.sbuf_tensor("at_sb", [128, 2, DC, TB], BF16))
                att = e3(nc.sbuf_tensor("att", [128, 1024], BF16))
                t1 = e3(nc.sbuf_tensor("t1", [128, 1024], F32))
                x1b = e3(nc.sbuf_tensor("x1b", [128, 2, 1024], F32))
                xb3 = e3(nc.sbuf_tensor("xb3", [128, 2, 1024], F32))
                h2b = e3(nc.sbuf_tensor("h2b", [128, 1024], BF16))
                h2tb = e3(nc.sbuf_tensor("h2tb", [128, 2, DC, 128], BF16))
                s3xs = e3(nc.sbuf_tensor("s3xs", [128, 1], F32))
                s3xq = e3(nc.sbuf_tensor("s3xq", [128, 1], F32))
                s3nm = e3(nc.sbuf_tensor("s3nm", [128, 1], F32))
                s3n2 = e3(nc.sbuf_tensor("s3n2", [128, 1], F32))
                s3vr = e3(nc.sbuf_tensor("s3vr", [128, 1], F32))
                s3lv = e3(nc.sbuf_tensor("s3lv", [128, 1], F32))
                s3rs2 = e3(nc.sbuf_tensor("s3rs2", [128, 1], F32))
                s3nm2 = e3(nc.sbuf_tensor("s3nm2", [128, 1], F32))
                s3rs3 = e3(nc.sbuf_tensor("s3rs3", [128, 1], F32))
                s3nm3 = e3(nc.sbuf_tensor("s3nm3", [128, 1], F32))
                ps_rb = e3(nc.psum_tensor("ps_rb", [128, 2, 512], F32))
                ps_wo = e3(nc.psum_tensor("ps_wo", [128, 2, 512], F32))
                ps_at = e3(nc.psum_tensor("ps_at", [128, 2, 1024], BF16))
                ps_h2 = e3(nc.psum_tensor("ps_h2", [128, 2, 1024], BF16))
                blockC = e3(nc.Block())

                syncC = []
                for e in range(NB):
                    syncC.append(("v", e))
                    for c4 in range(4):
                        gi3 = 4 * e + c4
                        syncC.append(("x", e, c4))
                        if gi3 >= 2:
                            syncC.append(("x1o", (gi3 - 2) // 4,
                                          (gi3 - 2) % 4))
                            syncC.append(("h2o", (gi3 - 2) // 4,
                                          (gi3 - 2) % 4))
                for gi3 in (30, 31):
                    syncC.append(("x1o", gi3 // 4, gi3 % 4))
                    syncC.append(("h2o", gi3 // 4, gi3 % 4))
                idxC = {t: i for i, t in enumerate(syncC)}
                baseA3 = cnt["sA"]

                def wA_C(eng, key):
                    if key[0] == "v":
                        W(eng, "sVld%d" % (key[1] % 2),
                          16 * (key[1] // 2 + 1))
                    elif key[0] == "x":
                        gi3 = 4 * key[1] + key[2]
                        W(eng, "sX3%d" % (gi3 % 2), 16 * (gi3 // 2 + 1))
                    elif key[0] == "x1o":
                        gi3 = 4 * key[1] + key[2]
                        W(eng, "sX1o%d" % (gi3 % 2), 16 * (gi3 // 2 + 1))
                    else:
                        gi3 = 4 * key[1] + key[2]
                        W(eng, "sH2o%d" % (gi3 % 2), 16 * (gi3 // 2 + 1))

                @blockC.sync
                def _(sync):
                    if "C" not in DBG_PHASES:
                        return
                    for t in syncC:
                        if t[0] == "v":
                            e = t[1]
                            W(sync, "s3_rvc", 8 * (e - 1))
                            I(sync.dma_start(
                                vt3[:, e % 2, :, :],
                                vsp_d[:, :, e * TB:(e + 1) * TB]
                                .rearrange("c p t -> p c t")),
                              "sVld%d" % (e % 2), 16)
                        elif t[0] == "x":
                            e, c4 = t[1], t[2]
                            gi3 = 4 * e + c4
                            W(sync, "s3_x1", gi3 - 1)
                            I(sync.dma_start(
                                xb3[:, gi3 % 2, :],
                                x_d[gi3 * 128:(gi3 + 1) * 128, :]),
                              "sX3%d" % (gi3 % 2), 16)
                        elif t[0] == "x1o":
                            e, c4 = t[1], t[2]
                            gi3 = 4 * e + c4
                            W(sync, "s3_x1", gi3 + 1)
                            I(sync.dma_start(
                                x1_d[gi3 * 128:(gi3 + 1) * 128, :],
                                x1b[:, gi3 % 2, :]),
                              "sX1o%d" % (gi3 % 2), 16)
                        else:
                            e, c4 = t[1], t[2]
                            gi3 = 4 * e + c4
                            W(sync, "s3_h2", gi3 + 1)
                            I(sync.dma_start(
                                h2t_d[:, :, gi3 * 128:(gi3 + 1) * 128]
                                .rearrange("c p t -> p c t"),
                                h2tb[:, gi3 % 2, :, :]),
                              "sH2o%d" % (gi3 % 2), 16)
                    for s in ["sVld0", "sVld1", "sX30", "sX31", "sX1o0",
                              "sX1o1", "sH2o0", "sH2o1"]:
                        sync.wait_ge(sem(s), cnt[s])

                @blockC.gpsimd
                def _(gpsimd):
                    if "C" not in DBG_PHASES:
                        return
                    I(gpsimd.dma_start(
                        wo_s[:, :, :],
                        wo_d.rearrange("(ci p) f -> p ci f", p=128)),
                      "sG3", 16)
                    I(gpsimd.dma_start(selr[:, :], selr_d), "sG3", 16)
                    I(gpsimd.dma_start(
                        wit[:, :, :],
                        wit_d.rearrange("(ci p) f -> p ci f", p=128)),
                      "sG3", 16)
                    I(gpsimd.dma_start(bcat[:, :], bcat_d), "sG3", 16)
                    gpsimd.wait_ge(sem("sG3"), cnt["sG3"])

                @blockC.tensor
                def _(tensor):
                    if "C" not in DBG_PHASES:
                        return
                    if DBG_ZERO_R:
                        W(tensor, "s_dbgz", 2)
                    W(tensor, "sG3", 16 * 4)
                    for e in range(NB):
                        rsrc = r_ev if e % 2 == 0 else r_od
                        rcol = (e // 2) * 512
                        for cc in range(DC):
                            ii = 8 * e + cc
                            W(tensor, "s3_rvc", ii - 1)
                            mm = tensor.matmul(
                                ps_rb[:, cc % 2, :],
                                selr[0:16, cc * 128:(cc + 1) * 128],
                                rsrc[0:16, rcol:rcol + 512],
                                start=True, stop=True)
                            I(mm, "s3_rb")
                        W(tensor, "s3_rvc", 8 * (e + 1))
                        for co in range(DC):
                            ii = 8 * e + co
                            W(tensor, "s3_ev", ii - 1)
                            for ci in range(DC):
                                mm = tensor.matmul(
                                    ps_wo[:, co % 2, :],
                                    wo_s[:, ci, co * 128:(co + 1) * 128],
                                    rv[:, e % 2, ci, :],
                                    start=(ci == 0), stop=(ci == DC - 1))
                            I(mm, "s3_wo")
                        for c4 in range(4):
                            gi3 = 4 * e + c4
                            W(tensor, "s3_ev", 8 * (e + 1))
                            W(tensor, "s3_ae", gi3 - 1)
                            for j in range(DC):
                                mm = tensor.matmul(
                                    ps_at[:, gi3 % 2,
                                          j * 128:(j + 1) * 128],
                                    at_sb[:, e % 2, j,
                                          c4 * 128:(c4 + 1) * 128],
                                    identb[:, :], is_transpose=True,
                                    start=(j == 0), stop=(j == DC - 1))
                            I(mm, "s3_at")
                            W(tensor, "s3_h2a", gi3 + 1)
                            W(tensor, "s3_h2", gi3 - 1)
                            for j in range(DC):
                                mm = tensor.matmul(
                                    ps_h2[:, gi3 % 2,
                                          j * 128:(j + 1) * 128],
                                    h2b[:, j * 128:(j + 1) * 128],
                                    identb[:, :], is_transpose=True,
                                    start=(j == 0), stop=(j == DC - 1))
                            I(mm, "s3_ath2")

                @blockC.scalar
                def _(scalar):
                    if "C" not in DBG_PHASES:
                        return
                    for e in range(NB):
                        for c4 in range(4):
                            gi3 = 4 * e + c4
                            W(scalar, "s3_var", 2 * gi3 + 1)
                            a0 = cnt["s_ac"]
                            I(scalar.activation(s3lv[:, 0:1], s3vr[:, 0:1],
                                                AF.Ln, bias=epsb[:, 0:1]),
                              "s_ac")
                            scalar.wait_ge(sem("s_ac"), a0 + 1)
                            I(scalar.activation(s3rs2[:, 0:1], s3lv[:, 0:1],
                                                AF.Exp, scale=-0.5),
                              "s3_rstd")
                            W(scalar, "s3_var", 2 * gi3 + 2)
                            a0 = cnt["s_ac"]
                            I(scalar.activation(s3lv[:, 0:1], s3vr[:, 0:1],
                                                AF.Ln, bias=epsb[:, 0:1]),
                              "s_ac")
                            scalar.wait_ge(sem("s_ac"), a0 + 1)
                            I(scalar.activation(s3rs3[:, 0:1], s3lv[:, 0:1],
                                                AF.Exp, scale=-0.5),
                              "s3_rstd")

                @blockC.vector
                def _(vector):
                    if "C" not in DBG_PHASES:
                        return
                    if DBG_ZERO_R:
                        I(vector.memset(r_ev[:, :], 0.0), "s_dbgz")
                        I(vector.memset(r_od[:, :], 0.0), "s_dbgz")
                        vector.wait_ge(sem("s_dbgz"), cnt["s_dbgz"])
                    for e in range(NB):
                        for cc in range(DC):
                            ii = 8 * e + cc
                            W(vector, "s3_rb", ii + 1)
                            wA_C(vector, ("v", e))
                            W(vector, "s3_wo", 8 * (e - 1))
                            I(vector.scalar_tensor_tensor(
                                rv[:, e % 2, cc, :], ps_rb[:, cc % 2, :],
                                1.0, vt3[:, e % 2, cc, :],
                                ALU.add, ALU.mult), "s3_rvc")
                        for co in range(DC):
                            ii = 8 * e + co
                            W(vector, "s3_wo", ii + 1)
                            W(vector, "s3_at", 4 * (e - 1))
                            I(vector.tensor_scalar(
                                at_sb[:, e % 2, co, :], ps_wo[:, co % 2, :],
                                bo[:, co:co + 1], None, ALU.add), "s3_ev")
                        for c4 in range(4):
                            gi3 = 4 * e + c4
                            W(vector, "s3_at", gi3 + 1)
                            I(vector.tensor_copy(att[:, :],
                                                 ps_at[:, gi3 % 2, :]),
                              "s3_ae")
                            vector.wait_ge(sem("s3_ae"), gi3 + 1)
                            W(vector, "s3_ath2", gi3)
                            v0 = cnt["s_dv"]
                            I(vector.reduce_sum(s3xs[:, 0:1], att[:, :],
                                                axis=AX.X), "s_dv")
                            I(vector.scalar_tensor_tensor(
                                h2b[:, :], att[:, :], 1.0, att[:, :],
                                ALU.mult, ALU.mult,
                                accum_out=s3xq[:, 0:1]), "s_dv")
                            vector.wait_ge(sem("s_dv"), v0 + 2)
                            I(vector.tensor_scalar(
                                s3nm[:, 0:1], s3xs[:, 0:1], -1.0 / D, None,
                                ALU.mult), "s_dv")
                            vector.wait_ge(sem("s_dv"), v0 + 3)
                            I(vector.tensor_mul(s3n2[:, 0:1], s3nm[:, 0:1],
                                                s3nm[:, 0:1]), "s_dv")
                            vector.wait_ge(sem("s_dv"), v0 + 4)
                            I(vector.scalar_tensor_tensor(
                                s3vr[:, 0:1], s3xq[:, 0:1], 1.0 / D,
                                s3n2[:, 0:1], ALU.mult, ALU.subtract),
                              "s3_var")
                            I(vector.tensor_copy(s3nm2[:, 0:1],
                                                 s3nm[:, 0:1]), "s_dv")
                            W(vector, "s3_rstd", 2 * gi3 + 1)
                            vector.wait_ge(sem("s_dv"), cnt["s_dv"])
                            I(vector.tensor_scalar(
                                t1[:, :], att[:, :], s3nm2[:, 0:1],
                                s3rs2[:, 0:1], ALU.add, ALU.mult), "s_dv")
                            wA_C(vector, ("x", e, c4))
                            if gi3 >= 2:
                                wA_C(vector, ("x1o", (gi3 - 2) // 4,
                                              (gi3 - 2) % 4))
                            vector.wait_ge(sem("s_dv"), cnt["s_dv"])
                            I(vector.tensor_add(x1b[:, gi3 % 2, :],
                                                t1[:, :],
                                                xb3[:, gi3 % 2, :]),
                              "s3_x1")
                            xv1 = x1b[:, gi3 % 2, :]
                            v0 = cnt["s_dv"]
                            vector.wait_ge(sem("s3_x1"), gi3 + 1)
                            I(vector.reduce_sum(s3xs[:, 0:1], xv1,
                                                axis=AX.X), "s_dv")
                            I(vector.scalar_tensor_tensor(
                                att[:, :], xv1, 1.0, xv1, ALU.mult,
                                ALU.mult, accum_out=s3xq[:, 0:1]), "s_dv")
                            vector.wait_ge(sem("s_dv"), v0 + 2)
                            I(vector.tensor_scalar(
                                s3nm[:, 0:1], s3xs[:, 0:1], -1.0 / D, None,
                                ALU.mult), "s_dv")
                            vector.wait_ge(sem("s_dv"), v0 + 3)
                            I(vector.tensor_mul(s3n2[:, 0:1], s3nm[:, 0:1],
                                                s3nm[:, 0:1]), "s_dv")
                            vector.wait_ge(sem("s_dv"), v0 + 4)
                            I(vector.scalar_tensor_tensor(
                                s3vr[:, 0:1], s3xq[:, 0:1], 1.0 / D,
                                s3n2[:, 0:1], ALU.mult, ALU.subtract),
                              "s3_var")
                            I(vector.tensor_copy(s3nm3[:, 0:1],
                                                 s3nm[:, 0:1]), "s_dv")
                            W(vector, "s3_rstd", 2 * gi3 + 2)
                            vector.wait_ge(sem("s_dv"), cnt["s_dv"])
                            I(vector.tensor_scalar(
                                h2b[:, :], xv1, s3nm3[:, 0:1],
                                s3rs3[:, 0:1], ALU.add, ALU.mult), "s3_h2a")
                            W(vector, "s3_ath2", gi3 + 1)
                            if gi3 >= 2:
                                wA_C(vector, ("h2o", (gi3 - 2) // 4,
                                              (gi3 - 2) % 4))
                            I(vector.tensor_copy(
                                h2tb[:, gi3 % 2, :, :],
                                ps_h2[:, gi3 % 2, :]
                                .rearrange("p (c t) -> p c t", t=128)),
                              "s3_h2")

            if DBG_DUMP:
                with nc.Block() as blkdbg3:
                    @blkdbg3.gpsimd
                    def _(gpsimd):
                        if "C" in DBG_PHASES:
                            I(gpsimd.dma_start(x1dump_d, x1_d), "sG", 16)
                            I(gpsimd.dma_start(h2dump_d, h2t_d), "sG", 16)
                        gpsimd.wait_ge(sem("sG"), cnt["sG"])

            with ExitStack() as sc4:
                e4 = sc4.enter_context
                h2t4 = e4(nc.sbuf_tensor("h2t4", [128, 2, DC, TB], BF16))
                lg_sb = e4(nc.sbuf_tensor("lg_sb", [8, TB], F32))
                lgt = e4(nc.sbuf_tensor("lgt", [128, 32], F32))
                mxv = e4(nc.sbuf_tensor("mxv", [128, 4], F32))
                mask_sb = e4(nc.sbuf_tensor("mask_sb", [128, 32], BF16))
                mkT = e4(nc.sbuf_tensor("mkT", [8, TB], BF16))
                xr_sb = e4(nc.sbuf_tensor("xr_sb", [32, TB], BF16))
                mxr = e4(nc.sbuf_tensor("mxr", [32, TB], BF16))
                interT = e4(nc.sbuf_tensor("interT", [128, FC, TB], BF16))
                wotb = e4(nc.sbuf_tensor("wotb", [128, 2, FC, 128], BF16))
                ft_sb = e4(nc.sbuf_tensor("ft_sb", [128, 2, DC, TB], BF16))
                ffb = e4(nc.sbuf_tensor("ffb", [128, 1024], BF16))
                t4 = e4(nc.sbuf_tensor("t4", [128, 1024], F32))
                yb = e4(nc.sbuf_tensor("yb", [128, 2, 1024], F32))
                x1b4 = e4(nc.sbuf_tensor("x1b4", [128, 2, 1024], F32))
                s4xs = e4(nc.sbuf_tensor("s4xs", [128, 1], F32))
                s4xq = e4(nc.sbuf_tensor("s4xq", [128, 1], F32))
                s4nm = e4(nc.sbuf_tensor("s4nm", [128, 1], F32))
                s4n2 = e4(nc.sbuf_tensor("s4n2", [128, 1], F32))
                s4vr = e4(nc.sbuf_tensor("s4vr", [128, 1], F32))
                s4lv = e4(nc.sbuf_tensor("s4lv", [128, 1], F32))
                s4rs = e4(nc.sbuf_tensor("s4rs", [128, 1], F32))
                ps_g = e4(nc.psum_tensor("ps_g", [128, 512], F32))
                ps_lgt = e4(nc.psum_tensor("ps_lgt", [128, 512], F32))
                ps_mt = e4(nc.psum_tensor("ps_mt", [8, 1024], BF16))
                ps_m32 = e4(nc.psum_tensor("ps_m32", [32, 512], F32))
                ps_xr = e4(nc.psum_tensor("ps_xr", [32, 512], F32))
                ps_wi = e4(nc.psum_tensor("ps_wi", [128, 2, 512], F32))
                ps_ft = e4(nc.psum_tensor("ps_ft", [128, 1024], BF16))
                blockD = e4(nc.Block())

                syncD = []
                for e in range(NB):
                    syncD.append(("h2i", e))
                    for c4 in range(4):
                        gi4 = 4 * e + c4
                        syncD.append(("x1i", e, c4))
                        if gi4 >= 2:
                            syncD.append(("yo", (gi4 - 2) // 4,
                                          (gi4 - 2) % 4))
                for gi4 in (30, 31):
                    syncD.append(("yo", gi4 // 4, gi4 % 4))
                idxD = {t: i for i, t in enumerate(syncD)}
                baseA4 = cnt["sA"]

                def wA_D(eng, key):
                    if key[0] == "h2i":
                        W(eng, "sH2i%d" % (key[1] % 2),
                          16 * (key[1] // 2 + 1))
                    elif key[0] == "x1i":
                        gi4 = 4 * key[1] + key[2]
                        W(eng, "sX1i%d" % (gi4 % 2), 16 * (gi4 // 2 + 1))
                    else:
                        gi4 = 4 * key[1] + key[2]
                        W(eng, "sYo%d" % (gi4 % 2), 16 * (gi4 // 2 + 1))

                @blockD.sync
                def _(sync):
                    if "D" not in DBG_PHASES:
                        return
                    for t in syncD:
                        if t[0] == "h2i":
                            e = t[1]
                            W(sync, "s4_womm", DC * (e - 1))
                            I(sync.dma_start(
                                h2t4[:, e % 2, :, :],
                                h2t_d[:, :, e * TB:(e + 1) * TB]
                                .rearrange("c p t -> p c t")),
                              "sH2i%d" % (e % 2), 16)
                        elif t[0] == "x1i":
                            e, c4 = t[1], t[2]
                            gi4 = 4 * e + c4
                            W(sync, "s4_y", gi4 - 1)
                            I(sync.dma_start(
                                x1b4[:, gi4 % 2, :],
                                x1_d[gi4 * 128:(gi4 + 1) * 128, :]),
                              "sX1i%d" % (gi4 % 2), 16)
                        else:
                            e, c4 = t[1], t[2]
                            gi4 = 4 * e + c4
                            W(sync, "s4_y", gi4 + 1)
                            I(sync.dma_start(
                                y_d[gi4 * 128:(gi4 + 1) * 128, :],
                                yb[:, gi4 % 2, :]),
                              "sYo%d" % (gi4 % 2), 16)
                    for s in ["sH2i0", "sH2i1", "sX1i0", "sX1i1",
                              "sYo0", "sYo1"]:
                        sync.wait_ge(sem(s), cnt[s])

                baseG_D = [0]

                @blockD.gpsimd
                def _(gpsimd):
                    if "D" not in DBG_PHASES:
                        return
                    for e in range(NB):
                        for co in range(DC):
                            oi = DC * e + co
                            W(gpsimd, "s4_womm", oi - 1)
                            I(gpsimd.dma_start(
                                wotb[:, co % 2, :, :],
                                wot_d.rearrange("(ci p) f -> p ci f",
                                                p=128)[:, :,
                                                       co * 128:
                                                       (co + 1) * 128]),
                              "sWot%d" % (oi % 2), 16)
                    gpsimd.wait_ge(sem("sWot0"), cnt["sWot0"])
                    gpsimd.wait_ge(sem("sWot1"), cnt["sWot1"])

                @blockD.tensor
                def _(tensor):
                    if "D" not in DBG_PHASES:
                        return
                    W(tensor, "sG3", 16 * 4)
                    for e in range(NB):
                        par = e % 2
                        wA_D(tensor, ("h2i", e))
                        W(tensor, "s4_lg", e)
                        for ci in range(DC):
                            mm = tensor.matmul(
                                ps_g[0:8, :], gwt[:, ci, :],
                                h2t4[:, par, ci, :],
                                start=(ci == 0), stop=(ci == DC - 1))
                        I(mm, "s4_gmm")
                        W(tensor, "s4_lg", e + 1)
                        W(tensor, "s4_lgt", e)
                        for c4 in range(4):
                            mm = tensor.matmul(
                                ps_lgt[:, c4 * 8:(c4 + 1) * 8],
                                lg_sb[0:8, c4 * 128:(c4 + 1) * 128],
                                identf8[:, :], is_transpose=True,
                                start=(c4 == 0), stop=(c4 == 3))
                        I(mm, "s4_lgtmm")
                        W(tensor, "s4_mkt", e)
                        for c4 in range(4):
                            W(tensor, "s4_mk", 4 * e + c4 + 1)
                            mm = tensor.matmul(
                                ps_mt[0:8, c4 * 128:(c4 + 1) * 128],
                                mask_sb[:, c4 * 8:(c4 + 1) * 8],
                                identb[:, :], is_transpose=True,
                                start=(c4 == 0), stop=(c4 == 3))
                        I(mm, "s4_mtr")
                        W(tensor, "s4_mkt", e + 1)
                        W(tensor, "s4_mx", e)
                        mm = tensor.matmul(ps_m32[:, :], selm[:, :],
                                           mkT[:, :], start=True, stop=True)
                        I(mm, "s4_m32mm")
                        W(tensor, "s4_xre", e)
                        for ci in range(DC):
                            mm = tensor.matmul(
                                ps_xr[:, :], acatt[:, ci, :],
                                h2t4[:, par, ci, :],
                                start=(ci == 0), stop=(ci == DC - 1))
                        I(mm, "s4_xrmm")
                        for fc in range(FC):
                            fi = FC * e + fc
                            W(tensor, "s4_ie", fi - 1)
                            if fc < 2:
                                W(tensor, "s4_fe", DC * e)
                                if fc == 0:
                                    W(tensor, "s4_mx", e + 1)
                            for ci in range(DC):
                                tensor.matmul(
                                    ps_wi[:, fc % 2, :],
                                    wit[:, ci, fc * 128:(fc + 1) * 128],
                                    h2t4[:, par, ci, :],
                                    start=(ci == 0), stop=False)
                            mm = tensor.matmul(
                                ps_wi[:, fc % 2, :],
                                bcat[:, fc * 128:(fc + 1) * 128],
                                mxr[:, :], start=False, stop=True)
                            I(mm, "s4_wimm")
                        W(tensor, "s4_ie", FC * (e + 1))
                        for co in range(DC):
                            oi = DC * e + co
                            W(tensor, "s4_fe", oi - 1)
                            W(tensor, "sWot%d" % (oi % 2),
                              16 * (oi // 2 + 1))
                            for fc in range(FC):
                                mm = tensor.matmul(
                                    ps_wi[:, co % 2, :],
                                    wotb[:, co % 2, fc, :],
                                    interT[:, fc, :],
                                    start=(fc == 0), stop=(fc == FC - 1))
                            I(mm, "s4_womm")
                        W(tensor, "s4_fe", DC * (e + 1))
                        for c4 in range(4):
                            gi4 = 4 * e + c4
                            W(tensor, "s4_ffe", gi4)
                            for j in range(DC):
                                mm = tensor.matmul(
                                    ps_ft[:, j * 128:(j + 1) * 128],
                                    ft_sb[:, par, j,
                                          c4 * 128:(c4 + 1) * 128],
                                    identb[:, :], is_transpose=True,
                                    start=(j == 0), stop=(j == DC - 1))
                            I(mm, "s4_ftr")

                @blockD.scalar
                def _(scalar):
                    if "D" not in DBG_PHASES:
                        return
                    for e in range(NB):
                        for c4 in range(4):
                            gi4 = 4 * e + c4
                            W(scalar, "s4_var", gi4 + 1)
                            a0 = cnt["s_ac"]
                            I(scalar.activation(s4lv[:, 0:1], s4vr[:, 0:1],
                                                AF.Ln, bias=epsb[:, 0:1]),
                              "s_ac")
                            scalar.wait_ge(sem("s_ac"), a0 + 1)
                            I(scalar.activation(s4rs[:, 0:1], s4lv[:, 0:1],
                                                AF.Exp, scale=-0.5),
                              "s4_rstd")

                @blockD.vector
                def _(vector):
                    if "D" not in DBG_PHASES:
                        return
                    for e in range(NB):
                        W(vector, "s4_gmm", e + 1)
                        W(vector, "s4_lgtmm", e)
                        I(vector.tensor_scalar(lg_sb[:, :], ps_g[0:8, :],
                                               gb[0:8, 0:1], None, ALU.add),
                          "s4_lg")
                        W(vector, "s4_lgtmm", e + 1)
                        I(vector.tensor_copy(lgt[:, :], ps_lgt[:, 0:32]),
                          "s4_lgt")
                        vector.wait_ge(sem("s4_lgt"), e + 1)
                        v0 = cnt["s_dv"]
                        for c4 in range(4):
                            I(vector.reduce_max(
                                mxv[:, c4:c4 + 1],
                                lgt[:, c4 * 8:(c4 + 1) * 8], axis=AX.X),
                              "s_dv")
                        vector.wait_ge(sem("s_dv"), v0 + 4)
                        for c4 in range(4):
                            I(vector.tensor_single_scalar(
                                mask_sb[:, c4 * 8:(c4 + 1) * 8],
                                lgt[:, c4 * 8:(c4 + 1) * 8],
                                mxv[:, c4:c4 + 1], ALU.is_equal), "s4_mk")
                        W(vector, "s4_mtr", e + 1)
                        W(vector, "s4_m32mm", e)
                        I(vector.tensor_copy(mkT[:, :], ps_mt[0:8, 0:512]),
                          "s4_mkt")
                        W(vector, "s4_xrmm", e + 1)
                        I(vector.tensor_copy(xr_sb[:, :], ps_xr[0:32, :]),
                          "s4_xre")
                        W(vector, "s4_m32mm", e + 1)
                        W(vector, "s4_wimm", FC * e)
                        vector.wait_ge(sem("s4_xre"), e + 1)
                        I(vector.tensor_tensor(mxr[:, :], xr_sb[:, :],
                                               ps_m32[0:32, :], ALU.mult),
                          "s4_mx")
                        if e >= 1:
                            W(vector, "s4_womm", DC * e)
                        for fc in range(FC):
                            fi = FC * e + fc
                            W(vector, "s4_wimm", fi + 1)
                            I(vector.tensor_scalar(
                                interT[:, fc, :], ps_wi[:, fc % 2, :],
                                wib[:, fc:fc + 1], 0.0, ALU.add, ALU.max),
                              "s4_ie")
                        for co in range(DC):
                            oi = DC * e + co
                            W(vector, "s4_womm", oi + 1)
                            W(vector, "s4_ftr", 4 * (e - 1))
                            I(vector.tensor_scalar(
                                ft_sb[:, e % 2, co, :], ps_wi[:, co % 2, :],
                                wob[:, co:co + 1], None, ALU.add), "s4_fe")
                        for c4 in range(4):
                            gi4 = 4 * e + c4
                            W(vector, "s4_ftr", gi4 + 1)
                            I(vector.tensor_copy(ffb[:, :], ps_ft[:, :]),
                              "s4_ffe")
                            vector.wait_ge(sem("s4_ffe"), gi4 + 1)
                            v0 = cnt["s_dv"]
                            I(vector.reduce_sum(s4xs[:, 0:1], ffb[:, :],
                                                axis=AX.X), "s_dv")
                            I(vector.scalar_tensor_tensor(
                                t4[:, :], ffb[:, :], 1.0, ffb[:, :],
                                ALU.mult, ALU.mult,
                                accum_out=s4xq[:, 0:1]), "s_dv")
                            vector.wait_ge(sem("s_dv"), v0 + 2)
                            I(vector.tensor_scalar(
                                s4nm[:, 0:1], s4xs[:, 0:1], -1.0 / D, None,
                                ALU.mult), "s_dv")
                            vector.wait_ge(sem("s_dv"), v0 + 3)
                            I(vector.tensor_mul(s4n2[:, 0:1], s4nm[:, 0:1],
                                                s4nm[:, 0:1]), "s_dv")
                            vector.wait_ge(sem("s_dv"), v0 + 4)
                            I(vector.scalar_tensor_tensor(
                                s4vr[:, 0:1], s4xq[:, 0:1], 1.0 / D,
                                s4n2[:, 0:1], ALU.mult, ALU.subtract),
                              "s4_var")
                            W(vector, "s4_rstd", gi4 + 1)
                            vector.wait_ge(sem("s_dv"), cnt["s_dv"])
                            I(vector.tensor_scalar(
                                t4[:, :], ffb[:, :], s4nm[:, 0:1],
                                s4rs[:, 0:1], ALU.add, ALU.mult), "s_dv")
                            wA_D(vector, ("x1i", e, c4))
                            if gi4 >= 2:
                                wA_D(vector, ("yo", (gi4 - 2) // 4,
                                              (gi4 - 2) % 4))
                            vector.wait_ge(sem("s_dv"), cnt["s_dv"])
                            I(vector.tensor_add(yb[:, gi4 % 2, :],
                                                t4[:, :],
                                                x1b4[:, gi4 % 2, :]),
                              "s4_y")

    return nc


_PROGRAM_CACHE = {}


def kernel(**inputs):
    X = np.ascontiguousarray(inputs["L0_attn_X_embed"], dtype=np.float32)
    Wq = np.ascontiguousarray(inputs["L0_attn_Wq"], dtype=np.float32)
    Wk = np.ascontiguousarray(inputs["L0_attn_Wk"], dtype=np.float32)
    Wv = np.ascontiguousarray(inputs["L0_attn_Wv"], dtype=np.float32)
    Wo = np.ascontiguousarray(inputs["L0_attn_Wo"], dtype=np.float32)

    def b128(v, cols):
        return np.ascontiguousarray(
            np.asarray(v, dtype=np.float32).reshape(cols, 128).T)

    bq = b128(inputs["L0_attn_bq"], 8)
    bk = b128(inputs["L0_attn_bk"], 8)
    bv = b128(inputs["L0_attn_bv"], 8)
    bo = b128(inputs["L0_attn_bo"], 8)
    wit = np.ascontiguousarray(
        np.asarray(inputs["L0_ffn_wi_W"], dtype=np.float32).T)
    wib = b128(inputs["L0_ffn_wi_b"], 32)
    wot = np.ascontiguousarray(
        np.asarray(inputs["L0_ffn_wo_W"], dtype=np.float32).T)
    wob = b128(inputs["L0_ffn_wo_b"], 8)
    gwt = np.ascontiguousarray(
        np.asarray(inputs["L0_ffn_gate_W"], dtype=np.float32).T)
    gbv = np.ascontiguousarray(
        np.asarray(inputs["L0_ffn_gate_b"], dtype=np.float32).reshape(8, 1))
    acatt = np.ascontiguousarray(
        np.asarray(inputs["L0_ffn_lora_A"], dtype=np.float32)
        .reshape(32, D).T)
    bcat = np.ascontiguousarray(
        np.asarray(inputs["L0_ffn_lora_B"], dtype=np.float32)
        .transpose(0, 2, 1).reshape(32, FF))
    ident = np.eye(128, dtype=np.float32)
    ones16 = np.ones((128, 16), dtype=np.float32)
    selr = np.zeros((16, 1024), dtype=np.float32)
    for cc in range(8):
        for m in range(128):
            selr[2 * cc + m // 64, cc * 128 + m] = 1.0
    epsc = np.stack([np.full(128, EPS, dtype=np.float32),
                     np.full(128, -0.75 * np.log(2.0),
                             dtype=np.float32)], axis=1)
    selm = np.zeros((8, 32), dtype=np.float32)
    for i in range(8):
        selm[i, 4 * i:4 * i + 4] = 1.0

    if "nc" not in _PROGRAM_CACHE:
        _PROGRAM_CACHE["nc"] = build_program()
    nc = _PROGRAM_CACHE["nc"]
    common = dict(wq=Wq, wk=Wk, wv=Wv, wo=Wo, bq=bq, bk=bk, bv=bv, bo=bo,
                  wit=wit, wib=wib, wot=wot, wob=wob, gwt=gwt, gb=gbv,
                  acatt=acatt, bcat=bcat, ident=ident, ones16=ones16,
                  selr=selr, selm=selm, epsc=epsc)
    in_maps = []
    for c in range(8):
        n, h = c // 2, c % 2
        xs = np.ascontiguousarray(X[n, h * TOK:(h + 1) * TOK, :])
        in_maps.append(dict(common, x=xs))
    res = run_bass_kernel_spmd(nc, in_maps, core_ids=list(range(8)))
    out = np.empty((4, 8192, D), dtype=np.float32)
    for c in range(8):
        n, h = c // 2, c % 2
        out[n, h * TOK:(h + 1) * TOK, :] = np.asarray(res.results[c]["y"])
    return out


# version 03
# speedup vs baseline: 1.0720x; 1.0720x over previous
import numpy as np
import concourse.bass as bass
import concourse.mybir as mybir
from concourse.bass_utils import run_bass_kernel_spmd
from contextlib import ExitStack

F32 = mybir.dt.float32
BF16 = mybir.dt.bfloat16
AF = mybir.ActivationFunctionType
ALU = mybir.AluOpType
AX = mybir.AxisListType

TOK = 4096
NB = 8
TB = 512
D = 1024
DC = 8
FF = 4096
FC = 32
NPAIR = 4
EPS = 1e-5
NCH = TOK // 128
NK = 256

DBG_PHASES = "ABCD"
DBG_DUMP = False
DBG_ZERO_R = False


def build_program():
    nc = bass.Bass("TRN2", target_bir_lowering=False)

    x_d = nc.dram_tensor("x", [TOK, D], F32, kind="ExternalInput").ap()
    wq_d = nc.dram_tensor("wq", [D, D], F32, kind="ExternalInput").ap()
    wk_d = nc.dram_tensor("wk", [D, D], F32, kind="ExternalInput").ap()
    wv_d = nc.dram_tensor("wv", [D, D], F32, kind="ExternalInput").ap()
    wo_d = nc.dram_tensor("wo", [D, D], F32, kind="ExternalInput").ap()
    bq_d = nc.dram_tensor("bq", [128, 8], F32, kind="ExternalInput").ap()
    bk_d = nc.dram_tensor("bk", [128, 8], F32, kind="ExternalInput").ap()
    bv_d = nc.dram_tensor("bv", [128, 8], F32, kind="ExternalInput").ap()
    bo_d = nc.dram_tensor("bo", [128, 8], F32, kind="ExternalInput").ap()
    wit_d = nc.dram_tensor("wit", [D, FF], F32, kind="ExternalInput").ap()
    wib_d = nc.dram_tensor("wib", [128, 32], F32, kind="ExternalInput").ap()
    wot_d = nc.dram_tensor("wot", [FF, D], F32, kind="ExternalInput").ap()
    wob_d = nc.dram_tensor("wob", [128, 8], F32, kind="ExternalInput").ap()
    gwt_d = nc.dram_tensor("gwt", [D, 8], F32, kind="ExternalInput").ap()
    gb_d = nc.dram_tensor("gb", [8, 1], F32, kind="ExternalInput").ap()
    acatt_d = nc.dram_tensor("acatt", [D, 32], F32, kind="ExternalInput").ap()
    bcat_d = nc.dram_tensor("bcat", [32, FF], F32, kind="ExternalInput").ap()
    ident_d = nc.dram_tensor("ident", [128, 128], F32, kind="ExternalInput").ap()
    ones16_d = nc.dram_tensor("ones16", [128, 16], F32, kind="ExternalInput").ap()
    selr_d = nc.dram_tensor("selr", [16, 1024], F32, kind="ExternalInput").ap()
    selm_d = nc.dram_tensor("selm", [8, 32], F32, kind="ExternalInput").ap()
    epsc_d = nc.dram_tensor("epsc", [128, 2], F32, kind="ExternalInput").ap()
    y_d = nc.dram_tensor("y", [TOK, D], F32, kind="ExternalOutput").ap()
    if DBG_DUMP:
        qdump_d = nc.dram_tensor("qdump", [128, 4, 8192], BF16,
                                 kind="ExternalOutput").ap()
        kdump_d = nc.dram_tensor("kdump", [128, 4, 8192], BF16,
                                 kind="ExternalOutput").ap()
        vdump_d = nc.dram_tensor("vdump", [DC, 128, TOK], BF16,
                                 kind="ExternalOutput").ap()
        rdump_d = nc.dram_tensor("rdump", [32, 2048], BF16,
                                 kind="ExternalOutput").ap()
        edump_d = nc.dram_tensor("edump", [128, 2, 8192], BF16,
                                 kind="ExternalOutput").ap()
        cadump_d = nc.dram_tensor("cadump", [128, 2, 8], F32,
                                  kind="ExternalOutput").ap()
        lhsdump_d = nc.dram_tensor("lhsdump", [128, 2, 1536], BF16,
                                   kind="ExternalOutput").ap()
        x1dump_d = nc.dram_tensor("x1dump", [TOK, D], F32,
                                  kind="ExternalOutput").ap()
        h2dump_d = nc.dram_tensor("h2dump", [DC, 128, TOK], BF16,
                                  kind="ExternalOutput").ap()
    vsp_d = nc.dram_tensor("vsp", [DC, 128, TOK], BF16).ap()
    x1_d = nc.dram_tensor("x1buf", [TOK, D], F32).ap()
    h2t_d = nc.dram_tensor("h2tbuf", [DC, 128, TOK], BF16).ap()

    with ExitStack() as top:
        e_ = top.enter_context
        bq = e_(nc.sbuf_tensor("bq_s", [128, 8], F32))
        bk = e_(nc.sbuf_tensor("bk_s", [128, 8], F32))
        bv = e_(nc.sbuf_tensor("bv_s", [128, 8], F32))
        bo = e_(nc.sbuf_tensor("bo_s", [128, 8], F32))
        wib = e_(nc.sbuf_tensor("wib_s", [128, 32], F32))
        wob = e_(nc.sbuf_tensor("wob_s", [128, 8], F32))
        gb = e_(nc.sbuf_tensor("gb_s", [8, 1], F32))
        identb = e_(nc.sbuf_tensor("identb", [128, 128], BF16))
        identf8 = e_(nc.sbuf_tensor("identf8", [8, 8], F32))
        ones16 = e_(nc.sbuf_tensor("ones16_s", [128, 16], BF16))
        selm = e_(nc.sbuf_tensor("selm_s", [8, 32], BF16))
        gwt = e_(nc.sbuf_tensor("gwt_s", [128, DC, 8], BF16))
        acatt = e_(nc.sbuf_tensor("acatt_s", [128, DC, 32], BF16))
        epsb = e_(nc.sbuf_tensor("epsb", [128, 2], F32))
        r_ev = e_(nc.sbuf_tensor("r_ev", [16, 2048], BF16))
        r_od = e_(nc.sbuf_tensor("r_od", [16, 2048], BF16))

        sems = {}
        cnt = {}

        def sem(name):
            if name not in sems:
                sems[name] = e_(nc.semaphore(name))
                cnt[name] = 0
            return sems[name]

        def W(eng, name, v):
            if v > 0:
                eng.wait_ge(sem(name), v)

        def I(instr, name, k=1):
            instr.then_inc(sem(name), k)
            cnt[name] += k
            return instr

        for _n in ["sA", "sG", "s_dv", "s_ac", "s_h", "s_var", "s_rstd",
                   "s_trev", "s_mmtr", "s_mmq", "s_qev", "s_vact",
                   "s_xfree", "s_qkvdone", "s_fill", "s_exp", "s_lhs",
                   "s_rmm", "s_rev",
                   "s3_rb", "s3_rvc", "s3_wo", "s3_ev", "s3_at", "s3_ae",
                   "s3_var", "s3_rstd", "s3_x1", "s3_h2", "s3_h2a",
                   "s3_ath2",
                   "s4_gmm", "s4_lg", "s4_lgtmm", "s4_lgt", "s4_mk",
                   "s4_mtr", "s4_mkt", "s4_m32mm", "s4_xre", "s4_xrmm",
                   "s4_mx", "s4_wimm", "s4_ie", "s4_womm", "s4_fe",
                   "s4_ftr", "s4_ffe", "s4_var", "s4_rstd", "s4_y", "s_eps",
                   "sX0", "sX1", "sVsp0", "sVsp1", "sG3",
                   "sVld0", "sVld1", "sX30", "sX31", "sX1o0", "sX1o1",
                   "sH2o0", "sH2o1", "sH2i0", "sH2i1", "sX1i0", "sX1i1",
                   "sYo0", "sYo1", "sWot0", "sWot1"]:
            sem(_n)

        with ExitStack() as scqt:
            eq_ = scqt.enter_context
            qt = eq_(nc.sbuf_tensor("qt", [128, NPAIR, 8192], BF16))
            kt = eq_(nc.sbuf_tensor("kt", [128, NPAIR, 8192], BF16))

            with ExitStack() as sc1:
                e1 = sc1.enter_context
                wq = e1(nc.sbuf_tensor("wq_s", [128, DC, D], BF16))
                wk = e1(nc.sbuf_tensor("wk_s", [128, DC, D], BF16))
                wv = e1(nc.sbuf_tensor("wv_s", [128, DC, D], BF16))
                xb = e1(nc.sbuf_tensor("xb", [128, 2, D], F32))
                hbf = e1(nc.sbuf_tensor("hbf", [128, 2, D], BF16))
                hT = e1(nc.sbuf_tensor("hT", [128, DC, TB], BF16))
                vt_tmp = e1(nc.sbuf_tensor("vt_tmp", [128, 2, TB], BF16))
                st_xs = e1(nc.sbuf_tensor("st_xs", [128, 1], F32))
                st_xq = e1(nc.sbuf_tensor("st_xq", [128, 1], F32))
                st_nm = e1(nc.sbuf_tensor("st_nm", [128, 1], F32))
                st_n2 = e1(nc.sbuf_tensor("st_n2", [128, 1], F32))
                st_vr = e1(nc.sbuf_tensor("st_vr", [128, 1], F32))
                st_lv = e1(nc.sbuf_tensor("st_lv", [128, 1], F32))
                st_rs = e1(nc.sbuf_tensor("st_rs", [128, 1], F32))
                ps_tr = e1(nc.psum_tensor("ps_tr", [128, 2, 1024], BF16))
                ps_qk = e1(nc.psum_tensor("ps_qk", [128, 2, 512], F32))
                blockA = e1(nc.Block())

                N_PRE_A = 9
                N_GPRE = 8

                @blockA.sync
                def _(sync):
                    for dst, src in [
                        (bq[:, :], bq_d), (bk[:, :], bk_d), (bv[:, :], bv_d),
                        (bo[:, :], bo_d), (wib[:, :], wib_d),
                        (wob[:, :], wob_d), (gb[:, :], gb_d),
                        (identf8[:, :], ident_d[0:8, 0:8]),
                        (epsb[:, :], epsc_d),
                    ]:
                        I(sync.dma_start(dst, src), "sA", 16)
                    if "A" not in DBG_PHASES:
                        sync.wait_ge(sem("sA"), cnt["sA"])
                        return
                    for gi in range(NCH):
                        W(sync, "s_xfree", gi - 1)
                        I(sync.dma_start(xb[:, gi % 2, :],
                                         x_d[gi * 128:(gi + 1) * 128, :]),
                          "sX%d" % (gi % 2), 16)

                @blockA.gpsimd
                def _(gpsimd):
                    loads = [
                        (wq[:, :, :],
                         wq_d.rearrange("(ci p) f -> p ci f", p=128)),
                        (wk[:, :, :],
                         wk_d.rearrange("(ci p) f -> p ci f", p=128)),
                        (wv[:, :, :],
                         wv_d.rearrange("(ci p) f -> p ci f", p=128)),
                        (identb[:, :], ident_d),
                        (ones16[:, :], ones16_d),
                        (selm[:, :], selm_d),
                        (gwt[:, :, :],
                         gwt_d.rearrange("(ci p) f -> p ci f", p=128)),
                        (acatt[:, :, :],
                         acatt_d.rearrange("(ci p) f -> p ci f", p=128)),
                    ]
                    assert len(loads) == N_GPRE
                    for dst, src in loads:
                        I(gpsimd.dma_start(dst, src), "sG", 16)
                    if "A" not in DBG_PHASES:
                        gpsimd.wait_ge(sem("sG"), cnt["sG"])
                        return
                    for e in range(NB):
                        for co in range(DC):
                            vidx = e * DC + co
                            W(gpsimd, "s_vact", vidx + 1)
                            I(gpsimd.dma_start(
                                vsp_d[co, :, e * TB:(e + 1) * TB],
                                vt_tmp[:, vidx % 2, :]),
                              "sVsp%d" % (vidx % 2), 16)
                    gpsimd.wait_ge(sem("sG"), cnt["sG"])
                    gpsimd.wait_ge(sem("sVsp0"), cnt["sVsp0"])
                    gpsimd.wait_ge(sem("sVsp1"), cnt["sVsp1"])

                @blockA.vector
                def _(vector):
                    if "A" not in DBG_PHASES:
                        return
                    for gi in range(NCH):
                        e = gi // 4
                        cc4 = gi % 4
                        W(vector, "sX%d" % (gi % 2), 16 * (gi // 2 + 1))
                        xv = xb[:, gi % 2, :]
                        W(vector, "s_mmtr", gi - 1)
                        v0 = cnt["s_dv"]
                        I(vector.reduce_sum(st_xs[:, 0:1], xv, axis=AX.X),
                          "s_dv")
                        I(vector.scalar_tensor_tensor(
                            hbf[:, gi % 2, :], xv, 1.0, xv, ALU.mult,
                            ALU.mult, accum_out=st_xq[:, 0:1]), "s_dv")
                        vector.wait_ge(sem("s_dv"), v0 + 2)
                        I(vector.tensor_scalar(st_nm[:, 0:1], st_xs[:, 0:1],
                                               -1.0 / D, None, ALU.mult),
                          "s_dv")
                        vector.wait_ge(sem("s_dv"), v0 + 3)
                        I(vector.tensor_mul(st_n2[:, 0:1], st_nm[:, 0:1],
                                            st_nm[:, 0:1]), "s_dv")
                        vector.wait_ge(sem("s_dv"), v0 + 4)
                        I(vector.scalar_tensor_tensor(
                            st_vr[:, 0:1], st_xq[:, 0:1], 1.0 / D,
                            st_n2[:, 0:1], ALU.mult, ALU.subtract), "s_var")
                        W(vector, "s_rstd", gi + 1)
                        I(vector.tensor_scalar(hbf[:, gi % 2, :], xv,
                                               st_nm[:, 0:1], st_rs[:, 0:1],
                                               ALU.add, ALU.mult), "s_h")
                        I(vector.engine_nop(), "s_xfree")
                        W(vector, "s_mmtr", gi + 1)
                        W(vector, "s_qkvdone", e)
                        I(vector.tensor_copy(
                            hT[:, :, cc4 * 128:(cc4 + 1) * 128],
                            ps_tr[:, gi % 2, :]
                            .rearrange("p (c t) -> p c t", t=128)), "s_trev")
                        if gi % 4 == 3:
                            p4 = e // 2
                            rh = (e % 2) * 64
                            for pi, (buf, bias) in enumerate(
                                    [(qt, bq), (kt, bk), (None, bv)]):
                                for co in range(DC):
                                    idx = e * 24 + pi * 8 + co
                                    W(vector, "s_mmq", idx + 1)
                                    pq = ps_qk[:, idx % 2, :]
                                    if pi < 2:
                                        I(vector.tensor_scalar(
                                            buf[rh:rh + 64, p4,
                                                2 * co * TB:
                                                (2 * co + 1) * TB],
                                            pq[0:64, :],
                                            bias[0:64, co:co + 1], None,
                                            ALU.add), "s_dv")
                                        I(vector.tensor_scalar(
                                            buf[rh:rh + 64, p4,
                                                (2 * co + 1) * TB:
                                                (2 * co + 2) * TB],
                                            pq[64:128, :],
                                            bias[64:128, co:co + 1], None,
                                            ALU.add), "s_qev")
                                    else:
                                        vidx = e * DC + co
                                        W(vector, "sVsp%d" % (vidx % 2),
                                          16 * (vidx // 2))
                                        I(vector.tensor_scalar(
                                            vt_tmp[:, vidx % 2, :],
                                            pq[:, :],
                                            bias[:, co:co + 1], None,
                                            ALU.add), "s_qev")
                                        I(vector.engine_nop(), "s_vact")

                @blockA.scalar
                def _(scalar):
                    if "A" not in DBG_PHASES:
                        return
                    W(scalar, "sA", 16 * N_PRE_A)
                    for gi in range(NCH):
                        W(scalar, "s_var", gi + 1)
                        W(scalar, "s_h", gi)
                        a0 = cnt["s_ac"]
                        I(scalar.activation(st_lv[:, 0:1], st_vr[:, 0:1],
                                            AF.Ln, bias=epsb[:, 0:1]), "s_ac")
                        scalar.wait_ge(sem("s_ac"), a0 + 1)
                        I(scalar.activation(st_rs[:, 0:1], st_lv[:, 0:1],
                                            AF.Exp, scale=-0.5), "s_rstd")

                @blockA.tensor
                def _(tensor):
                    if "A" not in DBG_PHASES:
                        return
                    W(tensor, "sG", 16 * N_GPRE)
                    for gi in range(NCH):
                        e = gi // 4
                        W(tensor, "s_h", gi + 1)
                        W(tensor, "s_trev", gi - 1)
                        for j in range(DC):
                            mm = tensor.matmul(
                                ps_tr[:, gi % 2, j * 128:(j + 1) * 128],
                                hbf[:, gi % 2, j * 128:(j + 1) * 128],
                                identb[:, :], is_transpose=True,
                                start=(j == 0), stop=(j == DC - 1))
                        I(mm, "s_mmtr")
                        if gi % 4 == 3:
                            W(tensor, "s_trev", 4 * (e + 1))
                            for pi, wmat in enumerate([wq, wk, wv]):
                                for co in range(DC):
                                    idx = e * 24 + pi * 8 + co
                                    W(tensor, "s_qev", idx - 1)
                                    pq = ps_qk[:, idx % 2, :]
                                    for ci in range(DC):
                                        mm = tensor.matmul(
                                            pq,
                                            wmat[:, ci,
                                                 co * 128:(co + 1) * 128],
                                            hT[:, ci, :],
                                            start=(ci == 0),
                                            stop=(ci == DC - 1))
                                    I(mm, "s_mmq")
                            I(tensor.nop(), "s_qkvdone")

            if DBG_DUMP:
                with nc.Block() as blkdbg1:
                    @blkdbg1.gpsimd
                    def _(gpsimd):
                        if "A" in DBG_PHASES:
                            I(gpsimd.dma_start(qdump_d, qt[:, :, :]),
                              "sG", 16)
                            I(gpsimd.dma_start(kdump_d, kt[:, :, :]),
                              "sG", 16)
                            I(gpsimd.dma_start(vdump_d, vsp_d), "sG", 16)
                        gpsimd.wait_ge(sem("sG"), cnt["sG"])

            with ExitStack() as sc2:
                e2 = sc2.enter_context
                eA = e2(nc.sbuf_tensor("eA", [128, 2, 8192], BF16))
                eB = e2(nc.sbuf_tensor("eB", [128, 2, 8192], BF16))
                lhsTall = e2(nc.sbuf_tensor("lhsTall", [128, 2, 512], BF16))
                caA = e2(nc.sbuf_tensor("caA", [128, 2, 8], F32))
                caB = e2(nc.sbuf_tensor("caB", [128, 2, 8], F32))
                cA = e2(nc.sbuf_tensor("cA", [128, 1], F32))
                cB = e2(nc.sbuf_tensor("cB", [128, 1], F32))
                icA = e2(nc.sbuf_tensor("icA", [128, 1], F32))
                icB = e2(nc.sbuf_tensor("icB", [128, 1], F32))
                icAb = e2(nc.sbuf_tensor("icAb", [128, 1], BF16))
                icBb = e2(nc.sbuf_tensor("icBb", [128, 1], BF16))
                icAr = e2(nc.sbuf_tensor("icAr", [128, 1], F32))
                icBr = e2(nc.sbuf_tensor("icBr", [128, 1], F32))
                ps_sl = e2(nc.psum_tensor("ps_sl", [128, 3, 1024], F32))
                rps = e2(nc.psum_tensor("rps", [48, 512], F32))
                blockB = e2(nc.Block())

                def lhsT_ap(par, j, head):
                    off = par * 512 + head * 256 + j * 16
                    return bass.AP(lhsTall, off, [[2 * 512, 128], [1, 16]])

                def lhsT_write_ap(par, head):
                    off = par * 512 + head * 256
                    return bass.AP(lhsTall, off, [[2 * 512, 128], [17, 16]])

                @blockB.tensor
                def _(tensor):
                    if "B" not in DBG_PHASES:
                        return
                    def emit_R(Kr):
                        if Kr < 0:
                            return
                        p, k = Kr // 64, Kr % 64
                        par = Kr % 2
                        W(tensor, "s_exp", (Kr + 1) * 16)
                        W(tensor, "s_lhs", Kr + 1)
                        if k == 0:
                            W(tensor, "s_rev", p)
                        for j in range(16):
                            mm = tensor.matmul(
                                rps[0:16, :], lhsT_ap(par, j, 0),
                                eA[:, par, j * 512:(j + 1) * 512],
                                start=(k == 0 and j == 0),
                                stop=(k == 63 and j == 15),
                                tile_position=(0, 0))
                            mm = tensor.matmul(
                                rps[32:48, :], lhsT_ap(par, j, 1),
                                eB[:, par, j * 512:(j + 1) * 512],
                                start=(k == 0 and j == 0),
                                stop=(k == 63 and j == 15),
                                tile_position=(0, 32))
                        I(mm, "s_rmm")

                    for K in range(NK):
                        p, k = K // 64, K % 64
                        for s2 in range(8):
                            gsA = K * 16 + 2 * s2
                            gsB = gsA + 1
                            slA = gsA % 3
                            slB = gsB % 3
                            W(tensor, "s_exp", gsB - 2)
                            lhsA = kt[0:64, p, k * 128:(k + 1) * 128]
                            lhsB = kt[64:128, p, k * 128:(k + 1) * 128]
                            mms = []
                            for half in range(2):
                                a0 = s2 * 1024 + half * 512
                                mms.append(tensor.matmul(
                                    ps_sl[:, slA,
                                          half * 512:(half + 1) * 512],
                                    lhsA, qt[0:64, p, a0:a0 + 512],
                                    start=True, stop=True,
                                    tile_position=(0, 0)))
                                mms.append(tensor.matmul(
                                    ps_sl[:, slB,
                                          half * 512:(half + 1) * 512],
                                    lhsB, qt[64:128, p, a0:a0 + 512],
                                    start=True, stop=True,
                                    tile_position=(64, 0)))
                            I(mms[2], "s_fill")
                            I(mms[3], "s_fill")
                        emit_R(K - 1)
                    emit_R(NK - 1)

                @blockB.scalar
                def _(scalar):
                    if "B" not in DBG_PHASES:
                        return
                    for K in range(NK):
                        par = K % 2
                        W(scalar, "s_rmm", K - 1)
                        W(scalar, "s_lhs", K - 1)
                        for s in range(16):
                            gs = K * 16 + s
                            head = s % 2
                            s2 = s // 2
                            sl = gs % 3
                            W(scalar, "s_fill", gs + 1)
                            ebuf = eA if head == 0 else eB
                            cab = caA if head == 0 else caB
                            I(scalar.activation(
                                ebuf[:, par, s2 * 1024:(s2 + 1) * 1024],
                                ps_sl[:, sl, :], AF.Exp,
                                scale=float(2.0 ** -10),
                                bias=epsb[:, 1:2],
                                accum_out=cab[:, par, s2:s2 + 1]), "s_exp")

                @blockB.vector
                def _(vector):
                    if "B" not in DBG_PHASES:
                        return
                    for K in range(NK):
                        par = K % 2
                        W(vector, "s_exp", (K + 1) * 16)
                        W(vector, "s_rmm", K - 1)
                        v0 = cnt["s_dv"]
                        I(vector.reduce_sum(cA[:, 0:1], caA[:, par, :],
                                            axis=AX.X), "s_dv")
                        I(vector.reduce_sum(cB[:, 0:1], caB[:, par, :],
                                            axis=AX.X), "s_dv")
                        vector.wait_ge(sem("s_dv"), v0 + 2)
                        I(vector.reciprocal(icA[:, 0:1], cA[:, 0:1]), "s_dv")
                        I(vector.reciprocal(icB[:, 0:1], cB[:, 0:1]), "s_dv")
                        I(vector.memset(lhsTall[:, par, :], 0.0), "s_dv")
                        vector.wait_ge(sem("s_dv"), v0 + 5)
                        I(vector.tensor_scalar(lhsT_write_ap(par, 0),
                                               ones16[:, :], icA[:, 0:1],
                                               None, ALU.mult), "s_dv")
                        I(vector.tensor_scalar(lhsT_write_ap(par, 1),
                                               ones16[:, :], icB[:, 0:1],
                                               None, ALU.mult), "s_lhs")
                        if K % 64 == 63:
                            pp = K // 64
                            W(vector, "s_rmm", K + 1)
                            I(vector.tensor_scalar(
                                r_ev[0:16, pp * 512:(pp + 1) * 512],
                                rps[0:16, :], -1.0, None, ALU.add), "s_dv")
                            I(vector.tensor_scalar(
                                r_od[0:16, pp * 512:(pp + 1) * 512],
                                rps[32:48, :], -1.0, None, ALU.add),
                              "s_rev")

        if DBG_DUMP:
            with nc.Block() as blkdbg2:
                @blkdbg2.gpsimd
                def _(gpsimd):
                    if "B" in DBG_PHASES:
                        I(gpsimd.dma_start(rdump_d[0:16, :], r_ev[:, :]),
                          "sG", 16)
                        I(gpsimd.dma_start(rdump_d[16:32, :], r_od[:, :]),
                          "sG", 16)
                        I(gpsimd.dma_start(edump_d, eA[:, :, :]), "sG", 16)
                        I(gpsimd.dma_start(cadump_d, caA[:, :, :]),
                          "sG", 16)
                        I(gpsimd.dma_start(lhsdump_d, lhsTall[:, :, :]),
                          "sG", 16)
                    gpsimd.wait_ge(sem("sG"), cnt["sG"])

        with ExitStack() as sc34:
            e34 = sc34.enter_context
            wit = e34(nc.sbuf_tensor("wit_s", [128, DC, FF], BF16))
            selr = e34(nc.sbuf_tensor("selr_s", [16, 1024], BF16))
            bcat = e34(nc.sbuf_tensor("bcat_s", [32, FF], BF16))
            baseG_C = [0]

            with ExitStack() as sc3:
                e3 = sc3.enter_context
                wo_s = e3(nc.sbuf_tensor("wo_s", [128, DC, D], BF16))
                vt3 = e3(nc.sbuf_tensor("vt3", [128, 2, DC, TB], BF16))
                rv = e3(nc.sbuf_tensor("rv", [128, 2, DC, TB], BF16))
                at_sb = e3(nc.sbuf_tensor("at_sb", [128, 2, DC, TB], BF16))
                att = e3(nc.sbuf_tensor("att", [128, 1024], BF16))
                t1 = e3(nc.sbuf_tensor("t1", [128, 1024], F32))
                x1b = e3(nc.sbuf_tensor("x1b", [128, 2, 1024], F32))
                xb3 = e3(nc.sbuf_tensor("xb3", [128, 2, 1024], F32))
                h2b = e3(nc.sbuf_tensor("h2b", [128, 1024], BF16))
                h2tb = e3(nc.sbuf_tensor("h2tb", [128, 2, DC, 128], BF16))
                s3xs = e3(nc.sbuf_tensor("s3xs", [128, 1], F32))
                s3xq = e3(nc.sbuf_tensor("s3xq", [128, 1], F32))
                s3nm = e3(nc.sbuf_tensor("s3nm", [128, 1], F32))
                s3n2 = e3(nc.sbuf_tensor("s3n2", [128, 1], F32))
                s3vr = e3(nc.sbuf_tensor("s3vr", [128, 1], F32))
                s3lv = e3(nc.sbuf_tensor("s3lv", [128, 1], F32))
                s3rs2 = e3(nc.sbuf_tensor("s3rs2", [128, 1], F32))
                s3nm2 = e3(nc.sbuf_tensor("s3nm2", [128, 1], F32))
                s3rs3 = e3(nc.sbuf_tensor("s3rs3", [128, 1], F32))
                s3nm3 = e3(nc.sbuf_tensor("s3nm3", [128, 1], F32))
                ps_rb = e3(nc.psum_tensor("ps_rb", [128, 2, 512], F32))
                ps_wo = e3(nc.psum_tensor("ps_wo", [128, 2, 512], F32))
                ps_at = e3(nc.psum_tensor("ps_at", [128, 2, 1024], BF16))
                ps_h2 = e3(nc.psum_tensor("ps_h2", [128, 2, 1024], BF16))
                blockC = e3(nc.Block())

                syncC = []
                for e in range(NB):
                    syncC.append(("v", e))
                    for c4 in range(4):
                        gi3 = 4 * e + c4
                        syncC.append(("x", e, c4))
                        if gi3 >= 2:
                            syncC.append(("x1o", (gi3 - 2) // 4,
                                          (gi3 - 2) % 4))
                            syncC.append(("h2o", (gi3 - 2) // 4,
                                          (gi3 - 2) % 4))
                for gi3 in (30, 31):
                    syncC.append(("x1o", gi3 // 4, gi3 % 4))
                    syncC.append(("h2o", gi3 // 4, gi3 % 4))
                idxC = {t: i for i, t in enumerate(syncC)}
                baseA3 = cnt["sA"]

                def wA_C(eng, key):
                    if key[0] == "v":
                        W(eng, "sVld%d" % (key[1] % 2),
                          16 * (key[1] // 2 + 1))
                    elif key[0] == "x":
                        gi3 = 4 * key[1] + key[2]
                        W(eng, "sX3%d" % (gi3 % 2), 16 * (gi3 // 2 + 1))
                    elif key[0] == "x1o":
                        gi3 = 4 * key[1] + key[2]
                        W(eng, "sX1o%d" % (gi3 % 2), 16 * (gi3 // 2 + 1))
                    else:
                        gi3 = 4 * key[1] + key[2]
                        W(eng, "sH2o%d" % (gi3 % 2), 16 * (gi3 // 2 + 1))

                @blockC.sync
                def _(sync):
                    if "C" not in DBG_PHASES:
                        return
                    for t in syncC:
                        if t[0] == "v":
                            e = t[1]
                            W(sync, "s3_rvc", 8 * (e - 1))
                            I(sync.dma_start(
                                vt3[:, e % 2, :, :],
                                vsp_d[:, :, e * TB:(e + 1) * TB]
                                .rearrange("c p t -> p c t")),
                              "sVld%d" % (e % 2), 16)
                        elif t[0] == "x":
                            e, c4 = t[1], t[2]
                            gi3 = 4 * e + c4
                            W(sync, "s3_x1", gi3 - 1)
                            I(sync.dma_start(
                                xb3[:, gi3 % 2, :],
                                x_d[gi3 * 128:(gi3 + 1) * 128, :]),
                              "sX3%d" % (gi3 % 2), 16)
                        elif t[0] == "x1o":
                            e, c4 = t[1], t[2]
                            gi3 = 4 * e + c4
                            W(sync, "s3_x1", gi3 + 1)
                            I(sync.dma_start(
                                x1_d[gi3 * 128:(gi3 + 1) * 128, :],
                                x1b[:, gi3 % 2, :]),
                              "sX1o%d" % (gi3 % 2), 16)
                        else:
                            e, c4 = t[1], t[2]
                            gi3 = 4 * e + c4
                            W(sync, "s3_h2", gi3 + 1)
                            I(sync.dma_start(
                                h2t_d[:, :, gi3 * 128:(gi3 + 1) * 128]
                                .rearrange("c p t -> p c t"),
                                h2tb[:, gi3 % 2, :, :]),
                              "sH2o%d" % (gi3 % 2), 16)
                    for s in ["sVld0", "sVld1", "sX30", "sX31", "sX1o0",
                              "sX1o1", "sH2o0", "sH2o1"]:
                        sync.wait_ge(sem(s), cnt[s])

                @blockC.gpsimd
                def _(gpsimd):
                    if "C" not in DBG_PHASES:
                        return
                    I(gpsimd.dma_start(
                        wo_s[:, :, :],
                        wo_d.rearrange("(ci p) f -> p ci f", p=128)),
                      "sG3", 16)
                    I(gpsimd.dma_start(selr[:, :], selr_d), "sG3", 16)
                    I(gpsimd.dma_start(
                        wit[:, :, :],
                        wit_d.rearrange("(ci p) f -> p ci f", p=128)),
                      "sG3", 16)
                    I(gpsimd.dma_start(bcat[:, :], bcat_d), "sG3", 16)
                    gpsimd.wait_ge(sem("sG3"), cnt["sG3"])

                @blockC.tensor
                def _(tensor):
                    if "C" not in DBG_PHASES:
                        return
                    if DBG_ZERO_R:
                        W(tensor, "s_dbgz", 2)
                    W(tensor, "sG3", 16 * 4)
                    for e in range(NB):
                        rsrc = r_ev if e % 2 == 0 else r_od
                        rcol = (e // 2) * 512
                        for cc in range(DC):
                            ii = 8 * e + cc
                            W(tensor, "s3_rvc", ii - 1)
                            mm = tensor.matmul(
                                ps_rb[:, cc % 2, :],
                                selr[0:16, cc * 128:(cc + 1) * 128],
                                rsrc[0:16, rcol:rcol + 512],
                                start=True, stop=True)
                            I(mm, "s3_rb")
                        W(tensor, "s3_rvc", 8 * (e + 1))
                        for co in range(DC):
                            ii = 8 * e + co
                            W(tensor, "s3_ev", ii - 1)
                            for ci in range(DC):
                                mm = tensor.matmul(
                                    ps_wo[:, co % 2, :],
                                    wo_s[:, ci, co * 128:(co + 1) * 128],
                                    rv[:, e % 2, ci, :],
                                    start=(ci == 0), stop=(ci == DC - 1))
                            I(mm, "s3_wo")
                        for c4 in range(4):
                            gi3 = 4 * e + c4
                            W(tensor, "s3_ev", 8 * (e + 1))
                            W(tensor, "s3_ae", gi3 - 1)
                            for j in range(DC):
                                mm = tensor.matmul(
                                    ps_at[:, gi3 % 2,
                                          j * 128:(j + 1) * 128],
                                    at_sb[:, e % 2, j,
                                          c4 * 128:(c4 + 1) * 128],
                                    identb[:, :], is_transpose=True,
                                    start=(j == 0), stop=(j == DC - 1))
                            I(mm, "s3_at")
                            W(tensor, "s3_h2a", gi3 + 1)
                            W(tensor, "s3_h2", gi3 - 1)
                            for j in range(DC):
                                mm = tensor.matmul(
                                    ps_h2[:, gi3 % 2,
                                          j * 128:(j + 1) * 128],
                                    h2b[:, j * 128:(j + 1) * 128],
                                    identb[:, :], is_transpose=True,
                                    start=(j == 0), stop=(j == DC - 1))
                            I(mm, "s3_ath2")

                @blockC.scalar
                def _(scalar):
                    if "C" not in DBG_PHASES:
                        return
                    for e in range(NB):
                        for c4 in range(4):
                            gi3 = 4 * e + c4
                            W(scalar, "s3_var", 2 * gi3 + 1)
                            a0 = cnt["s_ac"]
                            I(scalar.activation(s3lv[:, 0:1], s3vr[:, 0:1],
                                                AF.Ln, bias=epsb[:, 0:1]),
                              "s_ac")
                            scalar.wait_ge(sem("s_ac"), a0 + 1)
                            I(scalar.activation(s3rs2[:, 0:1], s3lv[:, 0:1],
                                                AF.Exp, scale=-0.5),
                              "s3_rstd")
                            W(scalar, "s3_var", 2 * gi3 + 2)
                            a0 = cnt["s_ac"]
                            I(scalar.activation(s3lv[:, 0:1], s3vr[:, 0:1],
                                                AF.Ln, bias=epsb[:, 0:1]),
                              "s_ac")
                            scalar.wait_ge(sem("s_ac"), a0 + 1)
                            I(scalar.activation(s3rs3[:, 0:1], s3lv[:, 0:1],
                                                AF.Exp, scale=-0.5),
                              "s3_rstd")

                @blockC.vector
                def _(vector):
                    if "C" not in DBG_PHASES:
                        return
                    if DBG_ZERO_R:
                        I(vector.memset(r_ev[:, :], 0.0), "s_dbgz")
                        I(vector.memset(r_od[:, :], 0.0), "s_dbgz")
                        vector.wait_ge(sem("s_dbgz"), cnt["s_dbgz"])
                    for e in range(NB):
                        for cc in range(DC):
                            ii = 8 * e + cc
                            W(vector, "s3_rb", ii + 1)
                            wA_C(vector, ("v", e))
                            W(vector, "s3_wo", 8 * (e - 1))
                            I(vector.scalar_tensor_tensor(
                                rv[:, e % 2, cc, :], ps_rb[:, cc % 2, :],
                                1.0, vt3[:, e % 2, cc, :],
                                ALU.add, ALU.mult), "s3_rvc")
                        for co in range(DC):
                            ii = 8 * e + co
                            W(vector, "s3_wo", ii + 1)
                            W(vector, "s3_at", 4 * (e - 1))
                            I(vector.tensor_scalar(
                                at_sb[:, e % 2, co, :], ps_wo[:, co % 2, :],
                                bo[:, co:co + 1], None, ALU.add), "s3_ev")
                        for c4 in range(4):
                            gi3 = 4 * e + c4
                            W(vector, "s3_at", gi3 + 1)
                            I(vector.tensor_copy(att[:, :],
                                                 ps_at[:, gi3 % 2, :]),
                              "s3_ae")
                            vector.wait_ge(sem("s3_ae"), gi3 + 1)
                            W(vector, "s3_ath2", gi3)
                            v0 = cnt["s_dv"]
                            I(vector.reduce_sum(s3xs[:, 0:1], att[:, :],
                                                axis=AX.X), "s_dv")
                            I(vector.scalar_tensor_tensor(
                                h2b[:, :], att[:, :], 1.0, att[:, :],
                                ALU.mult, ALU.mult,
                                accum_out=s3xq[:, 0:1]), "s_dv")
                            vector.wait_ge(sem("s_dv"), v0 + 2)
                            I(vector.tensor_scalar(
                                s3nm[:, 0:1], s3xs[:, 0:1], -1.0 / D, None,
                                ALU.mult), "s_dv")
                            vector.wait_ge(sem("s_dv"), v0 + 3)
                            I(vector.tensor_mul(s3n2[:, 0:1], s3nm[:, 0:1],
                                                s3nm[:, 0:1]), "s_dv")
                            vector.wait_ge(sem("s_dv"), v0 + 4)
                            I(vector.scalar_tensor_tensor(
                                s3vr[:, 0:1], s3xq[:, 0:1], 1.0 / D,
                                s3n2[:, 0:1], ALU.mult, ALU.subtract),
                              "s3_var")
                            I(vector.tensor_copy(s3nm2[:, 0:1],
                                                 s3nm[:, 0:1]), "s_dv")
                            W(vector, "s3_rstd", 2 * gi3 + 1)
                            vector.wait_ge(sem("s_dv"), cnt["s_dv"])
                            I(vector.tensor_scalar(
                                t1[:, :], att[:, :], s3nm2[:, 0:1],
                                s3rs2[:, 0:1], ALU.add, ALU.mult), "s_dv")
                            wA_C(vector, ("x", e, c4))
                            if gi3 >= 2:
                                wA_C(vector, ("x1o", (gi3 - 2) // 4,
                                              (gi3 - 2) % 4))
                            vector.wait_ge(sem("s_dv"), cnt["s_dv"])
                            I(vector.tensor_add(x1b[:, gi3 % 2, :],
                                                t1[:, :],
                                                xb3[:, gi3 % 2, :]),
                              "s3_x1")
                            xv1 = x1b[:, gi3 % 2, :]
                            v0 = cnt["s_dv"]
                            vector.wait_ge(sem("s3_x1"), gi3 + 1)
                            I(vector.reduce_sum(s3xs[:, 0:1], xv1,
                                                axis=AX.X), "s_dv")
                            I(vector.scalar_tensor_tensor(
                                att[:, :], xv1, 1.0, xv1, ALU.mult,
                                ALU.mult, accum_out=s3xq[:, 0:1]), "s_dv")
                            vector.wait_ge(sem("s_dv"), v0 + 2)
                            I(vector.tensor_scalar(
                                s3nm[:, 0:1], s3xs[:, 0:1], -1.0 / D, None,
                                ALU.mult), "s_dv")
                            vector.wait_ge(sem("s_dv"), v0 + 3)
                            I(vector.tensor_mul(s3n2[:, 0:1], s3nm[:, 0:1],
                                                s3nm[:, 0:1]), "s_dv")
                            vector.wait_ge(sem("s_dv"), v0 + 4)
                            I(vector.scalar_tensor_tensor(
                                s3vr[:, 0:1], s3xq[:, 0:1], 1.0 / D,
                                s3n2[:, 0:1], ALU.mult, ALU.subtract),
                              "s3_var")
                            I(vector.tensor_copy(s3nm3[:, 0:1],
                                                 s3nm[:, 0:1]), "s_dv")
                            W(vector, "s3_rstd", 2 * gi3 + 2)
                            vector.wait_ge(sem("s_dv"), cnt["s_dv"])
                            I(vector.tensor_scalar(
                                h2b[:, :], xv1, s3nm3[:, 0:1],
                                s3rs3[:, 0:1], ALU.add, ALU.mult), "s3_h2a")
                            W(vector, "s3_ath2", gi3 + 1)
                            if gi3 >= 2:
                                wA_C(vector, ("h2o", (gi3 - 2) // 4,
                                              (gi3 - 2) % 4))
                            I(vector.tensor_copy(
                                h2tb[:, gi3 % 2, :, :],
                                ps_h2[:, gi3 % 2, :]
                                .rearrange("p (c t) -> p c t", t=128)),
                              "s3_h2")

            if DBG_DUMP:
                with nc.Block() as blkdbg3:
                    @blkdbg3.gpsimd
                    def _(gpsimd):
                        if "C" in DBG_PHASES:
                            I(gpsimd.dma_start(x1dump_d, x1_d), "sG", 16)
                            I(gpsimd.dma_start(h2dump_d, h2t_d), "sG", 16)
                        gpsimd.wait_ge(sem("sG"), cnt["sG"])

            with ExitStack() as sc4:
                e4 = sc4.enter_context
                h2t4 = e4(nc.sbuf_tensor("h2t4", [128, 2, DC, TB], BF16))
                lg_sb = e4(nc.sbuf_tensor("lg_sb", [8, TB], F32))
                lgt = e4(nc.sbuf_tensor("lgt", [128, 32], F32))
                mxv = e4(nc.sbuf_tensor("mxv", [128, 4], F32))
                mask_sb = e4(nc.sbuf_tensor("mask_sb", [128, 32], BF16))
                mkT = e4(nc.sbuf_tensor("mkT", [8, TB], BF16))
                xr_sb = e4(nc.sbuf_tensor("xr_sb", [32, TB], BF16))
                mxr = e4(nc.sbuf_tensor("mxr", [32, TB], BF16))
                interT = e4(nc.sbuf_tensor("interT", [128, FC, TB], BF16))
                wotb = e4(nc.sbuf_tensor("wotb", [128, 2, FC, 128], BF16))
                ft_sb = e4(nc.sbuf_tensor("ft_sb", [128, 2, DC, TB], BF16))
                ffb = e4(nc.sbuf_tensor("ffb", [128, 1024], BF16))
                t4 = e4(nc.sbuf_tensor("t4", [128, 1024], F32))
                yb = e4(nc.sbuf_tensor("yb", [128, 2, 1024], F32))
                x1b4 = e4(nc.sbuf_tensor("x1b4", [128, 2, 1024], F32))
                s4xs = e4(nc.sbuf_tensor("s4xs", [128, 1], F32))
                s4xq = e4(nc.sbuf_tensor("s4xq", [128, 1], F32))
                s4nm = e4(nc.sbuf_tensor("s4nm", [128, 1], F32))
                s4n2 = e4(nc.sbuf_tensor("s4n2", [128, 1], F32))
                s4vr = e4(nc.sbuf_tensor("s4vr", [128, 1], F32))
                s4lv = e4(nc.sbuf_tensor("s4lv", [128, 1], F32))
                s4rs = e4(nc.sbuf_tensor("s4rs", [128, 1], F32))
                ps_g = e4(nc.psum_tensor("ps_g", [128, 512], F32))
                ps_lgt = e4(nc.psum_tensor("ps_lgt", [128, 512], F32))
                ps_mt = e4(nc.psum_tensor("ps_mt", [8, 1024], BF16))
                ps_m32 = e4(nc.psum_tensor("ps_m32", [32, 512], F32))
                ps_xr = e4(nc.psum_tensor("ps_xr", [32, 512], F32))
                ps_wi = e4(nc.psum_tensor("ps_wi", [128, 2, 512], F32))
                ps_ft = e4(nc.psum_tensor("ps_ft", [128, 1024], BF16))
                blockD = e4(nc.Block())

                syncD = []
                for e in range(NB):
                    syncD.append(("h2i", e))
                    for c4 in range(4):
                        gi4 = 4 * e + c4
                        syncD.append(("x1i", e, c4))
                        if gi4 >= 2:
                            syncD.append(("yo", (gi4 - 2) // 4,
                                          (gi4 - 2) % 4))
                for gi4 in (30, 31):
                    syncD.append(("yo", gi4 // 4, gi4 % 4))
                idxD = {t: i for i, t in enumerate(syncD)}
                baseA4 = cnt["sA"]

                def wA_D(eng, key):
                    if key[0] == "h2i":
                        W(eng, "sH2i%d" % (key[1] % 2),
                          16 * (key[1] // 2 + 1))
                    elif key[0] == "x1i":
                        gi4 = 4 * key[1] + key[2]
                        W(eng, "sX1i%d" % (gi4 % 2), 16 * (gi4 // 2 + 1))
                    else:
                        gi4 = 4 * key[1] + key[2]
                        W(eng, "sYo%d" % (gi4 % 2), 16 * (gi4 // 2 + 1))

                @blockD.sync
                def _(sync):
                    if "D" not in DBG_PHASES:
                        return
                    for t in syncD:
                        if t[0] == "h2i":
                            e = t[1]
                            W(sync, "s4_womm", DC * (e - 1))
                            I(sync.dma_start(
                                h2t4[:, e % 2, :, :],
                                h2t_d[:, :, e * TB:(e + 1) * TB]
                                .rearrange("c p t -> p c t")),
                              "sH2i%d" % (e % 2), 16)
                        elif t[0] == "x1i":
                            e, c4 = t[1], t[2]
                            gi4 = 4 * e + c4
                            W(sync, "s4_y", gi4 - 1)
                            I(sync.dma_start(
                                x1b4[:, gi4 % 2, :],
                                x1_d[gi4 * 128:(gi4 + 1) * 128, :]),
                              "sX1i%d" % (gi4 % 2), 16)
                        else:
                            e, c4 = t[1], t[2]
                            gi4 = 4 * e + c4
                            W(sync, "s4_y", gi4 + 1)
                            I(sync.dma_start(
                                y_d[gi4 * 128:(gi4 + 1) * 128, :],
                                yb[:, gi4 % 2, :]),
                              "sYo%d" % (gi4 % 2), 16)
                    for s in ["sH2i0", "sH2i1", "sX1i0", "sX1i1",
                              "sYo0", "sYo1"]:
                        sync.wait_ge(sem(s), cnt[s])

                baseG_D = [0]

                @blockD.gpsimd
                def _(gpsimd):
                    if "D" not in DBG_PHASES:
                        return
                    for e in range(NB):
                        for co in range(DC):
                            oi = DC * e + co
                            W(gpsimd, "s4_womm", oi - 1)
                            I(gpsimd.dma_start(
                                wotb[:, co % 2, :, :],
                                wot_d.rearrange("(ci p) f -> p ci f",
                                                p=128)[:, :,
                                                       co * 128:
                                                       (co + 1) * 128]),
                              "sWot%d" % (oi % 2), 16)
                    gpsimd.wait_ge(sem("sWot0"), cnt["sWot0"])
                    gpsimd.wait_ge(sem("sWot1"), cnt["sWot1"])

                @blockD.tensor
                def _(tensor):
                    if "D" not in DBG_PHASES:
                        return
                    W(tensor, "sG3", 16 * 4)
                    for e in range(NB):
                        par = e % 2
                        wA_D(tensor, ("h2i", e))
                        W(tensor, "s4_lg", e)
                        for ci in range(DC):
                            mm = tensor.matmul(
                                ps_g[0:8, :], gwt[:, ci, :],
                                h2t4[:, par, ci, :],
                                start=(ci == 0), stop=(ci == DC - 1))
                        I(mm, "s4_gmm")
                        W(tensor, "s4_lg", e + 1)
                        W(tensor, "s4_lgt", e)
                        for c4 in range(4):
                            mm = tensor.matmul(
                                ps_lgt[:, c4 * 8:(c4 + 1) * 8],
                                lg_sb[0:8, c4 * 128:(c4 + 1) * 128],
                                identf8[:, :], is_transpose=True,
                                start=(c4 == 0), stop=(c4 == 3))
                        I(mm, "s4_lgtmm")
                        W(tensor, "s4_mkt", e)
                        for c4 in range(4):
                            W(tensor, "s4_mk", 4 * e + c4 + 1)
                            mm = tensor.matmul(
                                ps_mt[0:8, c4 * 128:(c4 + 1) * 128],
                                mask_sb[:, c4 * 8:(c4 + 1) * 8],
                                identb[:, :], is_transpose=True,
                                start=(c4 == 0), stop=(c4 == 3))
                        I(mm, "s4_mtr")
                        W(tensor, "s4_mkt", e + 1)
                        W(tensor, "s4_mx", e)
                        mm = tensor.matmul(ps_m32[:, :], selm[:, :],
                                           mkT[:, :], start=True, stop=True)
                        I(mm, "s4_m32mm")
                        W(tensor, "s4_xre", e)
                        for ci in range(DC):
                            mm = tensor.matmul(
                                ps_xr[:, :], acatt[:, ci, :],
                                h2t4[:, par, ci, :],
                                start=(ci == 0), stop=(ci == DC - 1))
                        I(mm, "s4_xrmm")
                        for fc in range(FC):
                            fi = FC * e + fc
                            W(tensor, "s4_ie", fi - 1)
                            if fc < 2:
                                W(tensor, "s4_fe", DC * e)
                                if fc == 0:
                                    W(tensor, "s4_mx", e + 1)
                            for ci in range(DC):
                                tensor.matmul(
                                    ps_wi[:, fc % 2, :],
                                    wit[:, ci, fc * 128:(fc + 1) * 128],
                                    h2t4[:, par, ci, :],
                                    start=(ci == 0), stop=False)
                            mm = tensor.matmul(
                                ps_wi[:, fc % 2, :],
                                bcat[:, fc * 128:(fc + 1) * 128],
                                mxr[:, :], start=False, stop=True)
                            I(mm, "s4_wimm")
                        W(tensor, "s4_ie", FC * (e + 1))
                        for co in range(DC):
                            oi = DC * e + co
                            W(tensor, "s4_fe", oi - 1)
                            W(tensor, "sWot%d" % (oi % 2),
                              16 * (oi // 2 + 1))
                            for fc in range(FC):
                                mm = tensor.matmul(
                                    ps_wi[:, co % 2, :],
                                    wotb[:, co % 2, fc, :],
                                    interT[:, fc, :],
                                    start=(fc == 0), stop=(fc == FC - 1))
                            I(mm, "s4_womm")
                        W(tensor, "s4_fe", DC * (e + 1))
                        for c4 in range(4):
                            gi4 = 4 * e + c4
                            W(tensor, "s4_ffe", gi4)
                            for j in range(DC):
                                mm = tensor.matmul(
                                    ps_ft[:, j * 128:(j + 1) * 128],
                                    ft_sb[:, par, j,
                                          c4 * 128:(c4 + 1) * 128],
                                    identb[:, :], is_transpose=True,
                                    start=(j == 0), stop=(j == DC - 1))
                            I(mm, "s4_ftr")

                @blockD.scalar
                def _(scalar):
                    if "D" not in DBG_PHASES:
                        return
                    for e in range(NB):
                        for c4 in range(4):
                            gi4 = 4 * e + c4
                            W(scalar, "s4_var", gi4 + 1)
                            a0 = cnt["s_ac"]
                            I(scalar.activation(s4lv[:, 0:1], s4vr[:, 0:1],
                                                AF.Ln, bias=epsb[:, 0:1]),
                              "s_ac")
                            scalar.wait_ge(sem("s_ac"), a0 + 1)
                            I(scalar.activation(s4rs[:, 0:1], s4lv[:, 0:1],
                                                AF.Exp, scale=-0.5),
                              "s4_rstd")

                @blockD.vector
                def _(vector):
                    if "D" not in DBG_PHASES:
                        return
                    for e in range(NB):
                        W(vector, "s4_gmm", e + 1)
                        W(vector, "s4_lgtmm", e)
                        I(vector.tensor_scalar(lg_sb[:, :], ps_g[0:8, :],
                                               gb[0:8, 0:1], None, ALU.add),
                          "s4_lg")
                        W(vector, "s4_lgtmm", e + 1)
                        I(vector.tensor_copy(lgt[:, :], ps_lgt[:, 0:32]),
                          "s4_lgt")
                        vector.wait_ge(sem("s4_lgt"), e + 1)
                        v0 = cnt["s_dv"]
                        for c4 in range(4):
                            I(vector.reduce_max(
                                mxv[:, c4:c4 + 1],
                                lgt[:, c4 * 8:(c4 + 1) * 8], axis=AX.X),
                              "s_dv")
                        vector.wait_ge(sem("s_dv"), v0 + 4)
                        for c4 in range(4):
                            I(vector.tensor_single_scalar(
                                mask_sb[:, c4 * 8:(c4 + 1) * 8],
                                lgt[:, c4 * 8:(c4 + 1) * 8],
                                mxv[:, c4:c4 + 1], ALU.is_equal), "s4_mk")
                        W(vector, "s4_mtr", e + 1)
                        W(vector, "s4_m32mm", e)
                        I(vector.tensor_copy(mkT[:, :], ps_mt[0:8, 0:512]),
                          "s4_mkt")
                        W(vector, "s4_xrmm", e + 1)
                        I(vector.tensor_copy(xr_sb[:, :], ps_xr[0:32, :]),
                          "s4_xre")
                        W(vector, "s4_m32mm", e + 1)
                        W(vector, "s4_wimm", FC * e)
                        vector.wait_ge(sem("s4_xre"), e + 1)
                        I(vector.tensor_tensor(mxr[:, :], xr_sb[:, :],
                                               ps_m32[0:32, :], ALU.mult),
                          "s4_mx")
                        if e >= 1:
                            W(vector, "s4_womm", DC * e)
                        for fc in range(FC):
                            fi = FC * e + fc
                            W(vector, "s4_wimm", fi + 1)
                            I(vector.tensor_scalar(
                                interT[:, fc, :], ps_wi[:, fc % 2, :],
                                wib[:, fc:fc + 1], 0.0, ALU.add, ALU.max),
                              "s4_ie")
                        for co in range(DC):
                            oi = DC * e + co
                            W(vector, "s4_womm", oi + 1)
                            W(vector, "s4_ftr", 4 * (e - 1))
                            I(vector.tensor_scalar(
                                ft_sb[:, e % 2, co, :], ps_wi[:, co % 2, :],
                                wob[:, co:co + 1], None, ALU.add), "s4_fe")
                        for c4 in range(4):
                            gi4 = 4 * e + c4
                            W(vector, "s4_ftr", gi4 + 1)
                            I(vector.tensor_copy(ffb[:, :], ps_ft[:, :]),
                              "s4_ffe")
                            vector.wait_ge(sem("s4_ffe"), gi4 + 1)
                            v0 = cnt["s_dv"]
                            I(vector.reduce_sum(s4xs[:, 0:1], ffb[:, :],
                                                axis=AX.X), "s_dv")
                            I(vector.scalar_tensor_tensor(
                                t4[:, :], ffb[:, :], 1.0, ffb[:, :],
                                ALU.mult, ALU.mult,
                                accum_out=s4xq[:, 0:1]), "s_dv")
                            vector.wait_ge(sem("s_dv"), v0 + 2)
                            I(vector.tensor_scalar(
                                s4nm[:, 0:1], s4xs[:, 0:1], -1.0 / D, None,
                                ALU.mult), "s_dv")
                            vector.wait_ge(sem("s_dv"), v0 + 3)
                            I(vector.tensor_mul(s4n2[:, 0:1], s4nm[:, 0:1],
                                                s4nm[:, 0:1]), "s_dv")
                            vector.wait_ge(sem("s_dv"), v0 + 4)
                            I(vector.scalar_tensor_tensor(
                                s4vr[:, 0:1], s4xq[:, 0:1], 1.0 / D,
                                s4n2[:, 0:1], ALU.mult, ALU.subtract),
                              "s4_var")
                            W(vector, "s4_rstd", gi4 + 1)
                            vector.wait_ge(sem("s_dv"), cnt["s_dv"])
                            I(vector.tensor_scalar(
                                t4[:, :], ffb[:, :], s4nm[:, 0:1],
                                s4rs[:, 0:1], ALU.add, ALU.mult), "s_dv")
                            wA_D(vector, ("x1i", e, c4))
                            if gi4 >= 2:
                                wA_D(vector, ("yo", (gi4 - 2) // 4,
                                              (gi4 - 2) % 4))
                            vector.wait_ge(sem("s_dv"), cnt["s_dv"])
                            I(vector.tensor_add(yb[:, gi4 % 2, :],
                                                t4[:, :],
                                                x1b4[:, gi4 % 2, :]),
                              "s4_y")

    return nc


_PROGRAM_CACHE = {}


def kernel(**inputs):
    X = np.ascontiguousarray(inputs["L0_attn_X_embed"], dtype=np.float32)
    Wq = np.ascontiguousarray(inputs["L0_attn_Wq"], dtype=np.float32)
    Wk = np.ascontiguousarray(inputs["L0_attn_Wk"], dtype=np.float32)
    Wv = np.ascontiguousarray(inputs["L0_attn_Wv"], dtype=np.float32)
    Wo = np.ascontiguousarray(inputs["L0_attn_Wo"], dtype=np.float32)

    def b128(v, cols):
        return np.ascontiguousarray(
            np.asarray(v, dtype=np.float32).reshape(cols, 128).T)

    bq = b128(inputs["L0_attn_bq"], 8)
    bk = b128(inputs["L0_attn_bk"], 8)
    bv = b128(inputs["L0_attn_bv"], 8)
    bo = b128(inputs["L0_attn_bo"], 8)
    wit = np.ascontiguousarray(
        np.asarray(inputs["L0_ffn_wi_W"], dtype=np.float32).T)
    wib = b128(inputs["L0_ffn_wi_b"], 32)
    wot = np.ascontiguousarray(
        np.asarray(inputs["L0_ffn_wo_W"], dtype=np.float32).T)
    wob = b128(inputs["L0_ffn_wo_b"], 8)
    gwt = np.ascontiguousarray(
        np.asarray(inputs["L0_ffn_gate_W"], dtype=np.float32).T)
    gbv = np.ascontiguousarray(
        np.asarray(inputs["L0_ffn_gate_b"], dtype=np.float32).reshape(8, 1))
    acatt = np.ascontiguousarray(
        np.asarray(inputs["L0_ffn_lora_A"], dtype=np.float32)
        .reshape(32, D).T)
    bcat = np.ascontiguousarray(
        np.asarray(inputs["L0_ffn_lora_B"], dtype=np.float32)
        .transpose(0, 2, 1).reshape(32, FF))
    ident = np.eye(128, dtype=np.float32)
    ones16 = np.ones((128, 16), dtype=np.float32)
    selr = np.zeros((16, 1024), dtype=np.float32)
    for cc in range(8):
        for m in range(128):
            selr[2 * cc + m // 64, cc * 128 + m] = 1.0
    epsc = np.stack([np.full(128, EPS, dtype=np.float32),
                     np.full(128, -0.75 * np.log(2.0),
                             dtype=np.float32)], axis=1)
    selm = np.zeros((8, 32), dtype=np.float32)
    for i in range(8):
        selm[i, 4 * i:4 * i + 4] = 1.0

    if "nc" not in _PROGRAM_CACHE:
        _PROGRAM_CACHE["nc"] = build_program()
    nc = _PROGRAM_CACHE["nc"]
    common = dict(wq=Wq, wk=Wk, wv=Wv, wo=Wo, bq=bq, bk=bk, bv=bv, bo=bo,
                  wit=wit, wib=wib, wot=wot, wob=wob, gwt=gwt, gb=gbv,
                  acatt=acatt, bcat=bcat, ident=ident, ones16=ones16,
                  selr=selr, selm=selm, epsc=epsc)
    in_maps = []
    for c in range(8):
        n, h = c // 2, c % 2
        xs = np.ascontiguousarray(X[n, h * TOK:(h + 1) * TOK, :])
        in_maps.append(dict(common, x=xs))
    res = run_bass_kernel_spmd(nc, in_maps, core_ids=list(range(8)))
    out = np.empty((4, 8192, D), dtype=np.float32)
    for c in range(8):
        n, h = c // 2, c % 2
        out[n, h * TOK:(h + 1) * TOK, :] = np.asarray(res.results[c]["y"])
    return out


# version 06
# speedup vs baseline: 1.1393x; 1.0628x over previous
import numpy as np
import concourse.bass as bass
import concourse.mybir as mybir
from concourse.bass_utils import run_bass_kernel_spmd
from contextlib import ExitStack

F32 = mybir.dt.float32
BF16 = mybir.dt.bfloat16
AF = mybir.ActivationFunctionType
ALU = mybir.AluOpType
AX = mybir.AxisListType

TOK = 4096
NB = 8
TB = 512
D = 1024
DC = 8
FF = 4096
FC = 32
NPAIR = 4
EPS = 1e-5
NCH = TOK // 128
NK = 256

DBG_PHASES = "ABCD"
DBG_DUMP = False
DBG_ZERO_R = False


def build_program():
    nc = bass.Bass("TRN2", target_bir_lowering=False)

    x_d = nc.dram_tensor("x", [TOK, D], F32, kind="ExternalInput").ap()
    wq_d = nc.dram_tensor("wq", [D, D], F32, kind="ExternalInput").ap()
    wk_d = nc.dram_tensor("wk", [D, D], F32, kind="ExternalInput").ap()
    wv_d = nc.dram_tensor("wv", [D, D], F32, kind="ExternalInput").ap()
    wo_d = nc.dram_tensor("wo", [D, D], F32, kind="ExternalInput").ap()
    bq_d = nc.dram_tensor("bq", [128, 8], F32, kind="ExternalInput").ap()
    bk_d = nc.dram_tensor("bk", [128, 8], F32, kind="ExternalInput").ap()
    bv_d = nc.dram_tensor("bv", [128, 8], F32, kind="ExternalInput").ap()
    bo_d = nc.dram_tensor("bo", [128, 8], F32, kind="ExternalInput").ap()
    wit_d = nc.dram_tensor("wit", [D, FF], F32, kind="ExternalInput").ap()
    wib_d = nc.dram_tensor("wib", [128, 32], F32, kind="ExternalInput").ap()
    wot_d = nc.dram_tensor("wot", [FF, D], F32, kind="ExternalInput").ap()
    wob_d = nc.dram_tensor("wob", [128, 8], F32, kind="ExternalInput").ap()
    gwt_d = nc.dram_tensor("gwt", [D, 8], F32, kind="ExternalInput").ap()
    gb_d = nc.dram_tensor("gb", [8, 1], F32, kind="ExternalInput").ap()
    acatt_d = nc.dram_tensor("acatt", [D, 32], F32, kind="ExternalInput").ap()
    bcat_d = nc.dram_tensor("bcat", [32, FF], F32, kind="ExternalInput").ap()
    ident_d = nc.dram_tensor("ident", [128, 128], F32, kind="ExternalInput").ap()
    ones16_d = nc.dram_tensor("ones16", [128, 16], F32, kind="ExternalInput").ap()
    selr_d = nc.dram_tensor("selr", [16, 1024], F32, kind="ExternalInput").ap()
    selm_d = nc.dram_tensor("selm", [8, 32], F32, kind="ExternalInput").ap()
    epsc_d = nc.dram_tensor("epsc", [128, 2], F32, kind="ExternalInput").ap()
    y_d = nc.dram_tensor("y", [TOK, D], F32, kind="ExternalOutput").ap()
    if DBG_DUMP:
        qdump_d = nc.dram_tensor("qdump", [128, 4, 8192], BF16,
                                 kind="ExternalOutput").ap()
        kdump_d = nc.dram_tensor("kdump", [128, 4, 8192], BF16,
                                 kind="ExternalOutput").ap()
        vdump_d = nc.dram_tensor("vdump", [DC, 128, TOK], BF16,
                                 kind="ExternalOutput").ap()
        rdump_d = nc.dram_tensor("rdump", [32, 2048], BF16,
                                 kind="ExternalOutput").ap()
        edump_d = nc.dram_tensor("edump", [128, 2, 8192], BF16,
                                 kind="ExternalOutput").ap()
        cadump_d = nc.dram_tensor("cadump", [128, 2, 8], F32,
                                  kind="ExternalOutput").ap()
        lhsdump_d = nc.dram_tensor("lhsdump", [128, 2, 1536], BF16,
                                   kind="ExternalOutput").ap()
        x1dump_d = nc.dram_tensor("x1dump", [TOK, D], F32,
                                  kind="ExternalOutput").ap()
        h2dump_d = nc.dram_tensor("h2dump", [DC, 128, TOK], BF16,
                                  kind="ExternalOutput").ap()
    vsp_d = nc.dram_tensor("vsp", [DC, 128, TOK], BF16).ap()
    x1_d = nc.dram_tensor("x1buf", [TOK, D], F32).ap()
    h2t_d = nc.dram_tensor("h2tbuf", [DC, 128, TOK], BF16).ap()

    with ExitStack() as top:
        e_ = top.enter_context
        bq = e_(nc.sbuf_tensor("bq_s", [128, 8], F32))
        bk = e_(nc.sbuf_tensor("bk_s", [128, 8], F32))
        bv = e_(nc.sbuf_tensor("bv_s", [128, 8], F32))
        bo = e_(nc.sbuf_tensor("bo_s", [128, 8], F32))
        wib = e_(nc.sbuf_tensor("wib_s", [128, 32], F32))
        wob = e_(nc.sbuf_tensor("wob_s", [128, 8], F32))
        gb = e_(nc.sbuf_tensor("gb_s", [8, 1], F32))
        identb = e_(nc.sbuf_tensor("identb", [128, 128], BF16))
        identf8 = e_(nc.sbuf_tensor("identf8", [8, 8], F32))
        ones16 = e_(nc.sbuf_tensor("ones16_s", [128, 16], BF16))
        selm = e_(nc.sbuf_tensor("selm_s", [8, 32], BF16))
        gwt = e_(nc.sbuf_tensor("gwt_s", [128, DC, 8], BF16))
        acatt = e_(nc.sbuf_tensor("acatt_s", [128, DC, 32], BF16))
        epsb = e_(nc.sbuf_tensor("epsb", [128, 2], F32))
        r_ev = e_(nc.sbuf_tensor("r_ev", [16, 2048], BF16))
        r_od = e_(nc.sbuf_tensor("r_od", [16, 2048], BF16))

        sems = {}
        cnt = {}

        def sem(name):
            if name not in sems:
                sems[name] = e_(nc.semaphore(name))
                cnt[name] = 0
            return sems[name]

        def W(eng, name, v):
            if v > 0:
                eng.wait_ge(sem(name), v)

        def I(instr, name, k=1):
            instr.then_inc(sem(name), k)
            cnt[name] += k
            return instr

        for _n in ["sA", "sG", "s_dv", "s_ac", "s_h", "s_var", "s_rstd",
                   "s_trev", "s_mmtr", "s_mmq", "s_qev", "s_vact",
                   "s_xfree", "s_qkvdone", "s_fill", "s_exp", "s_lhs",
                   "s_rmm", "s_rev",
                   "s3_rb", "s3_rvc", "s3_wo", "s3_ev", "s3_at", "s3_ae",
                   "s3_var", "s3_rstd", "s3_x1", "s3_h2", "s3_h2a",
                   "s3_ath2",
                   "s4_gmm", "s4_lg", "s4_lgtmm", "s4_lgt", "s4_mk",
                   "s4_mtr", "s4_mkt", "s4_m32mm", "s4_xre", "s4_xrmm",
                   "s4_mx", "s4_wimm", "s4_ie", "s4_womm", "s4_fe",
                   "s4_ftr", "s4_ffe", "s4_var", "s4_rstd", "s4_y", "s_eps",
                   "sX0", "sX1", "sVsp0", "sVsp1", "sG3",
                   "sVld0", "sVld1", "sX30", "sX31", "sX1o0", "sX1o1",
                   "sH2o0", "sH2o1", "sH2i0", "sH2i1", "sX1i0", "sX1i1",
                   "sYo0", "sYo1", "sWot0", "sWot1"]:
            sem(_n)

        with ExitStack() as scqt:
            eq_ = scqt.enter_context
            qt = eq_(nc.sbuf_tensor("qt", [128, NPAIR, 8192], BF16))
            kt = eq_(nc.sbuf_tensor("kt", [128, NPAIR, 8192], BF16))

            with ExitStack() as sc1:
                e1 = sc1.enter_context
                wq = e1(nc.sbuf_tensor("wq_s", [128, DC, D], BF16))
                wk = e1(nc.sbuf_tensor("wk_s", [128, DC, D], BF16))
                wv = e1(nc.sbuf_tensor("wv_s", [128, DC, D], BF16))
                xb = e1(nc.sbuf_tensor("xb", [128, 2, D], F32))
                hbf = e1(nc.sbuf_tensor("hbf", [128, 2, D], BF16))
                hT = e1(nc.sbuf_tensor("hT", [128, DC, TB], BF16))
                vt_tmp = e1(nc.sbuf_tensor("vt_tmp", [128, 2, TB], BF16))
                st_xs = e1(nc.sbuf_tensor("st_xs", [128, 1], F32))
                st_xq = e1(nc.sbuf_tensor("st_xq", [128, 1], F32))
                st_nm = e1(nc.sbuf_tensor("st_nm", [128, 1], F32))
                st_n2 = e1(nc.sbuf_tensor("st_n2", [128, 1], F32))
                st_vr = e1(nc.sbuf_tensor("st_vr", [128, 1], F32))
                st_lv = e1(nc.sbuf_tensor("st_lv", [128, 1], F32))
                st_rs = e1(nc.sbuf_tensor("st_rs", [128, 1], F32))
                ps_tr = e1(nc.psum_tensor("ps_tr", [128, 2, 1024], BF16))
                ps_qk = e1(nc.psum_tensor("ps_qk", [128, 2, 512], F32))
                blockA = e1(nc.Block())

                N_PRE_A = 9
                N_GPRE = 8

                @blockA.sync
                def _(sync):
                    for dst, src in [
                        (bq[:, :], bq_d), (bk[:, :], bk_d), (bv[:, :], bv_d),
                        (bo[:, :], bo_d), (wib[:, :], wib_d),
                        (wob[:, :], wob_d), (gb[:, :], gb_d),
                        (identf8[:, :], ident_d[0:8, 0:8]),
                        (epsb[:, :], epsc_d),
                    ]:
                        I(sync.dma_start(dst, src), "sA", 16)
                    if "A" not in DBG_PHASES:
                        sync.wait_ge(sem("sA"), cnt["sA"])
                        return
                    for gi in range(NCH):
                        W(sync, "s_xfree", gi - 1)
                        I(sync.dma_start(xb[:, gi % 2, :],
                                         x_d[gi * 128:(gi + 1) * 128, :]),
                          "sX%d" % (gi % 2), 16)

                @blockA.gpsimd
                def _(gpsimd):
                    loads = [
                        (wq[:, :, :],
                         wq_d.rearrange("(ci p) f -> p ci f", p=128)),
                        (wk[:, :, :],
                         wk_d.rearrange("(ci p) f -> p ci f", p=128)),
                        (wv[:, :, :],
                         wv_d.rearrange("(ci p) f -> p ci f", p=128)),
                        (identb[:, :], ident_d),
                        (ones16[:, :], ones16_d),
                        (selm[:, :], selm_d),
                        (gwt[:, :, :],
                         gwt_d.rearrange("(ci p) f -> p ci f", p=128)),
                        (acatt[:, :, :],
                         acatt_d.rearrange("(ci p) f -> p ci f", p=128)),
                    ]
                    assert len(loads) == N_GPRE
                    for dst, src in loads:
                        I(gpsimd.dma_start(dst, src), "sG", 16)
                    if "A" not in DBG_PHASES:
                        gpsimd.wait_ge(sem("sG"), cnt["sG"])
                        return
                    for e in range(NB):
                        for co in range(DC):
                            vidx = e * DC + co
                            W(gpsimd, "s_vact", vidx + 1)
                            I(gpsimd.dma_start(
                                vsp_d[co, :, e * TB:(e + 1) * TB],
                                vt_tmp[:, vidx % 2, :]),
                              "sVsp%d" % (vidx % 2), 16)
                    gpsimd.wait_ge(sem("sG"), cnt["sG"])
                    gpsimd.wait_ge(sem("sVsp0"), cnt["sVsp0"])
                    gpsimd.wait_ge(sem("sVsp1"), cnt["sVsp1"])

                @blockA.vector
                def _(vector):
                    if "A" not in DBG_PHASES:
                        return
                    for gi in range(NCH):
                        e = gi // 4
                        cc4 = gi % 4
                        W(vector, "sX%d" % (gi % 2), 16 * (gi // 2 + 1))
                        xv = xb[:, gi % 2, :]
                        W(vector, "s_mmtr", gi - 1)
                        v0 = cnt["s_dv"]
                        I(vector.reduce_sum(st_xs[:, 0:1], xv, axis=AX.X),
                          "s_dv")
                        I(vector.scalar_tensor_tensor(
                            hbf[:, gi % 2, :], xv, 1.0, xv, ALU.mult,
                            ALU.mult, accum_out=st_xq[:, 0:1]), "s_dv")
                        vector.wait_ge(sem("s_dv"), v0 + 2)
                        I(vector.tensor_scalar(st_nm[:, 0:1], st_xs[:, 0:1],
                                               -1.0 / D, None, ALU.mult),
                          "s_dv")
                        vector.wait_ge(sem("s_dv"), v0 + 3)
                        I(vector.tensor_mul(st_n2[:, 0:1], st_nm[:, 0:1],
                                            st_nm[:, 0:1]), "s_dv")
                        vector.wait_ge(sem("s_dv"), v0 + 4)
                        I(vector.scalar_tensor_tensor(
                            st_vr[:, 0:1], st_xq[:, 0:1], 1.0 / D,
                            st_n2[:, 0:1], ALU.mult, ALU.subtract), "s_var")
                        W(vector, "s_rstd", gi + 1)
                        I(vector.tensor_scalar(hbf[:, gi % 2, :], xv,
                                               st_nm[:, 0:1], st_rs[:, 0:1],
                                               ALU.add, ALU.mult), "s_h")
                        I(vector.engine_nop(), "s_xfree")
                        W(vector, "s_mmtr", gi + 1)
                        W(vector, "s_qkvdone", e)
                        I(vector.tensor_copy(
                            hT[:, :, cc4 * 128:(cc4 + 1) * 128],
                            ps_tr[:, gi % 2, :]
                            .rearrange("p (c t) -> p c t", t=128)), "s_trev")
                        if gi % 4 == 3:
                            p4 = e // 2
                            rh = (e % 2) * 64
                            for pi, (buf, bias) in enumerate(
                                    [(qt, bq), (kt, bk), (None, bv)]):
                                for co in range(DC):
                                    idx = e * 24 + pi * 8 + co
                                    W(vector, "s_mmq", idx + 1)
                                    pq = ps_qk[:, idx % 2, :]
                                    if pi < 2:
                                        I(vector.tensor_scalar(
                                            buf[rh:rh + 64, p4,
                                                2 * co * TB:
                                                (2 * co + 1) * TB],
                                            pq[0:64, :],
                                            bias[0:64, co:co + 1], None,
                                            ALU.add), "s_dv")
                                        I(vector.tensor_scalar(
                                            buf[rh:rh + 64, p4,
                                                (2 * co + 1) * TB:
                                                (2 * co + 2) * TB],
                                            pq[64:128, :],
                                            bias[64:128, co:co + 1], None,
                                            ALU.add), "s_qev")
                                    else:
                                        vidx = e * DC + co
                                        W(vector, "sVsp%d" % (vidx % 2),
                                          16 * (vidx // 2))
                                        I(vector.tensor_scalar(
                                            vt_tmp[:, vidx % 2, :],
                                            pq[:, :],
                                            bias[:, co:co + 1], None,
                                            ALU.add), "s_qev")
                                        I(vector.engine_nop(), "s_vact")

                @blockA.scalar
                def _(scalar):
                    if "A" not in DBG_PHASES:
                        return
                    W(scalar, "sA", 16 * N_PRE_A)
                    for gi in range(NCH):
                        W(scalar, "s_var", gi + 1)
                        W(scalar, "s_h", gi)
                        a0 = cnt["s_ac"]
                        I(scalar.activation(st_lv[:, 0:1], st_vr[:, 0:1],
                                            AF.Ln, bias=epsb[:, 0:1]), "s_ac")
                        scalar.wait_ge(sem("s_ac"), a0 + 1)
                        I(scalar.activation(st_rs[:, 0:1], st_lv[:, 0:1],
                                            AF.Exp, scale=-0.5), "s_rstd")

                @blockA.tensor
                def _(tensor):
                    if "A" not in DBG_PHASES:
                        return
                    W(tensor, "sG", 16 * N_GPRE)
                    for gi in range(NCH):
                        e = gi // 4
                        W(tensor, "s_h", gi + 1)
                        W(tensor, "s_trev", gi - 1)
                        for j in range(DC):
                            mm = tensor.matmul(
                                ps_tr[:, gi % 2, j * 128:(j + 1) * 128],
                                hbf[:, gi % 2, j * 128:(j + 1) * 128],
                                identb[:, :], is_transpose=True,
                                start=(j == 0), stop=(j == DC - 1))
                        I(mm, "s_mmtr")
                        if gi % 4 == 3:
                            W(tensor, "s_trev", 4 * (e + 1))
                            for pi, wmat in enumerate([wq, wk, wv]):
                                for co in range(DC):
                                    idx = e * 24 + pi * 8 + co
                                    W(tensor, "s_qev", idx - 1)
                                    pq = ps_qk[:, idx % 2, :]
                                    for ci in range(DC):
                                        mm = tensor.matmul(
                                            pq,
                                            wmat[:, ci,
                                                 co * 128:(co + 1) * 128],
                                            hT[:, ci, :],
                                            start=(ci == 0),
                                            stop=(ci == DC - 1))
                                    I(mm, "s_mmq")
                            I(tensor.nop(), "s_qkvdone")

            if DBG_DUMP:
                with nc.Block() as blkdbg1:
                    @blkdbg1.gpsimd
                    def _(gpsimd):
                        if "A" in DBG_PHASES:
                            I(gpsimd.dma_start(qdump_d, qt[:, :, :]),
                              "sG", 16)
                            I(gpsimd.dma_start(kdump_d, kt[:, :, :]),
                              "sG", 16)
                            I(gpsimd.dma_start(vdump_d, vsp_d), "sG", 16)
                        gpsimd.wait_ge(sem("sG"), cnt["sG"])

            with ExitStack() as sc2:
                e2 = sc2.enter_context
                eA = e2(nc.sbuf_tensor("eA", [128, 2, 8192], BF16))
                eB = e2(nc.sbuf_tensor("eB", [128, 2, 8192], BF16))
                lhsTall = e2(nc.sbuf_tensor("lhsTall", [128, 2, 512], BF16))
                caA = e2(nc.sbuf_tensor("caA", [128, 2, 8], F32))
                caB = e2(nc.sbuf_tensor("caB", [128, 2, 8], F32))
                cA = e2(nc.sbuf_tensor("cA", [128, 1], F32))
                cB = e2(nc.sbuf_tensor("cB", [128, 1], F32))
                icA = e2(nc.sbuf_tensor("icA", [128, 1], F32))
                icB = e2(nc.sbuf_tensor("icB", [128, 1], F32))
                icAb = e2(nc.sbuf_tensor("icAb", [128, 1], BF16))
                icBb = e2(nc.sbuf_tensor("icBb", [128, 1], BF16))
                icAr = e2(nc.sbuf_tensor("icAr", [128, 1], F32))
                icBr = e2(nc.sbuf_tensor("icBr", [128, 1], F32))
                ps_sl = e2(nc.psum_tensor("ps_sl", [128, 3, 1024], F32))
                rps = e2(nc.psum_tensor("rps", [48, 512], F32))
                blockB = e2(nc.Block())

                def lhsT_ap(par, j, head):
                    off = par * 512 + head * 256 + j * 16
                    return bass.AP(lhsTall, off, [[2 * 512, 128], [1, 16]])

                def lhsT_write_ap(par, head):
                    off = par * 512 + head * 256
                    return bass.AP(lhsTall, off, [[2 * 512, 128], [17, 16]])

                @blockB.tensor
                def _(tensor):
                    if "B" not in DBG_PHASES:
                        return
                    def emit_R(Kr):
                        if Kr < 0:
                            return
                        p, k = Kr // 64, Kr % 64
                        par = Kr % 2
                        W(tensor, "s_exp", (Kr + 1) * 16)
                        W(tensor, "s_lhs", Kr + 1)
                        if k == 0:
                            W(tensor, "s_rev", p)
                        for j in range(16):
                            mm = tensor.matmul(
                                rps[0:16, :], lhsT_ap(par, j, 0),
                                eA[:, par, j * 512:(j + 1) * 512],
                                start=(k == 0 and j == 0),
                                stop=(k == 63 and j == 15),
                                tile_position=(0, 0))
                            mm = tensor.matmul(
                                rps[32:48, :], lhsT_ap(par, j, 1),
                                eB[:, par, j * 512:(j + 1) * 512],
                                start=(k == 0 and j == 0),
                                stop=(k == 63 and j == 15),
                                tile_position=(0, 32))
                        I(mm, "s_rmm")

                    for K in range(NK):
                        p, k = K // 64, K % 64
                        for s2 in range(8):
                            gsA = K * 16 + 2 * s2
                            gsB = gsA + 1
                            slA = gsA % 3
                            slB = gsB % 3
                            W(tensor, "s_exp", gsB - 2)
                            lhsA = kt[0:64, p, k * 128:(k + 1) * 128]
                            lhsB = kt[64:128, p, k * 128:(k + 1) * 128]
                            mms = []
                            for half in range(2):
                                a0 = s2 * 1024 + half * 512
                                mms.append(tensor.matmul(
                                    ps_sl[:, slA,
                                          half * 512:(half + 1) * 512],
                                    lhsA, qt[0:64, p, a0:a0 + 512],
                                    start=True, stop=True,
                                    tile_position=(0, 0)))
                                mms.append(tensor.matmul(
                                    ps_sl[:, slB,
                                          half * 512:(half + 1) * 512],
                                    lhsB, qt[64:128, p, a0:a0 + 512],
                                    start=True, stop=True,
                                    tile_position=(64, 0)))
                            I(mms[2], "s_fill")
                            I(mms[3], "s_fill")
                        emit_R(K - 1)
                    emit_R(NK - 1)

                @blockB.scalar
                def _(scalar):
                    if "B" not in DBG_PHASES:
                        return
                    for K in range(NK):
                        par = K % 2
                        W(scalar, "s_rmm", K - 1)
                        W(scalar, "s_lhs", K - 1)
                        for s in range(16):
                            gs = K * 16 + s
                            head = s % 2
                            s2 = s // 2
                            sl = gs % 3
                            W(scalar, "s_fill", gs + 1)
                            ebuf = eA if head == 0 else eB
                            cab = caA if head == 0 else caB
                            I(scalar.activation(
                                ebuf[:, par, s2 * 1024:(s2 + 1) * 1024],
                                ps_sl[:, sl, :], AF.Exp,
                                scale=float(2.0 ** -10),
                                bias=epsb[:, 1:2]), "s_exp")

                @blockB.vector
                def _(vector):
                    if "B" not in DBG_PHASES:
                        return
                    for K in range(NK):
                        par = K % 2
                        W(vector, "s_exp", (K + 1) * 16)
                        W(vector, "s_rmm", K - 1)
                        v0 = cnt["s_dv"]
                        I(vector.tensor_scalar(eA[:, par, :], eA[:, par, :],
                                               1.0, 0.0, ALU.mult, ALU.add,
                                               accum_out=cA[:, 0:1]), "s_dv")
                        I(vector.tensor_scalar(eB[:, par, :], eB[:, par, :],
                                               1.0, 0.0, ALU.mult, ALU.add,
                                               accum_out=cB[:, 0:1]), "s_dv")
                        vector.wait_ge(sem("s_dv"), v0 + 2)
                        I(vector.reciprocal(icA[:, 0:1], cA[:, 0:1]), "s_dv")
                        I(vector.reciprocal(icB[:, 0:1], cB[:, 0:1]), "s_dv")
                        I(vector.memset(lhsTall[:, par, :], 0.0), "s_dv")
                        vector.wait_ge(sem("s_dv"), v0 + 5)
                        I(vector.tensor_scalar(lhsT_write_ap(par, 0),
                                               ones16[:, :], icA[:, 0:1],
                                               None, ALU.mult), "s_dv")
                        I(vector.tensor_scalar(lhsT_write_ap(par, 1),
                                               ones16[:, :], icB[:, 0:1],
                                               None, ALU.mult), "s_lhs")
                        if K % 64 == 63:
                            pp = K // 64
                            W(vector, "s_rmm", K + 1)
                            I(vector.tensor_scalar(
                                r_ev[0:16, pp * 512:(pp + 1) * 512],
                                rps[0:16, :], -1.0, None, ALU.add), "s_dv")
                            I(vector.tensor_scalar(
                                r_od[0:16, pp * 512:(pp + 1) * 512],
                                rps[32:48, :], -1.0, None, ALU.add),
                              "s_rev")

        if DBG_DUMP:
            with nc.Block() as blkdbg2:
                @blkdbg2.gpsimd
                def _(gpsimd):
                    if "B" in DBG_PHASES:
                        I(gpsimd.dma_start(rdump_d[0:16, :], r_ev[:, :]),
                          "sG", 16)
                        I(gpsimd.dma_start(rdump_d[16:32, :], r_od[:, :]),
                          "sG", 16)
                        I(gpsimd.dma_start(edump_d, eA[:, :, :]), "sG", 16)
                        I(gpsimd.dma_start(cadump_d, caA[:, :, :]),
                          "sG", 16)
                        I(gpsimd.dma_start(lhsdump_d, lhsTall[:, :, :]),
                          "sG", 16)
                    gpsimd.wait_ge(sem("sG"), cnt["sG"])

        with ExitStack() as sc34:
            e34 = sc34.enter_context
            wit = e34(nc.sbuf_tensor("wit_s", [128, DC, FF], BF16))
            selr = e34(nc.sbuf_tensor("selr_s", [16, 1024], BF16))
            bcat = e34(nc.sbuf_tensor("bcat_s", [32, FF], BF16))
            baseG_C = [0]

            with ExitStack() as sc3:
                e3 = sc3.enter_context
                wo_s = e3(nc.sbuf_tensor("wo_s", [128, DC, D], BF16))
                vt3 = e3(nc.sbuf_tensor("vt3", [128, 2, DC, TB], BF16))
                rv = e3(nc.sbuf_tensor("rv", [128, 2, DC, TB], BF16))
                at_sb = e3(nc.sbuf_tensor("at_sb", [128, 2, DC, TB], BF16))
                att = e3(nc.sbuf_tensor("att", [128, 1024], BF16))
                t1 = e3(nc.sbuf_tensor("t1", [128, 1024], F32))
                x1b = e3(nc.sbuf_tensor("x1b", [128, 2, 1024], F32))
                xb3 = e3(nc.sbuf_tensor("xb3", [128, 2, 1024], F32))
                h2b = e3(nc.sbuf_tensor("h2b", [128, 1024], BF16))
                h2tb = e3(nc.sbuf_tensor("h2tb", [128, 2, DC, 128], BF16))
                s3xs = e3(nc.sbuf_tensor("s3xs", [128, 1], F32))
                s3xq = e3(nc.sbuf_tensor("s3xq", [128, 1], F32))
                s3nm = e3(nc.sbuf_tensor("s3nm", [128, 1], F32))
                s3n2 = e3(nc.sbuf_tensor("s3n2", [128, 1], F32))
                s3vr = e3(nc.sbuf_tensor("s3vr", [128, 1], F32))
                s3lv = e3(nc.sbuf_tensor("s3lv", [128, 1], F32))
                s3rs2 = e3(nc.sbuf_tensor("s3rs2", [128, 1], F32))
                s3nm2 = e3(nc.sbuf_tensor("s3nm2", [128, 1], F32))
                s3rs3 = e3(nc.sbuf_tensor("s3rs3", [128, 1], F32))
                s3nm3 = e3(nc.sbuf_tensor("s3nm3", [128, 1], F32))
                ps_rb = e3(nc.psum_tensor("ps_rb", [128, 2, 512], F32))
                ps_wo = e3(nc.psum_tensor("ps_wo", [128, 2, 512], F32))
                ps_at = e3(nc.psum_tensor("ps_at", [128, 2, 1024], BF16))
                ps_h2 = e3(nc.psum_tensor("ps_h2", [128, 2, 1024], BF16))
                blockC = e3(nc.Block())

                syncC = []
                for e in range(NB):
                    syncC.append(("v", e))
                    for c4 in range(4):
                        gi3 = 4 * e + c4
                        syncC.append(("x", e, c4))
                        if gi3 >= 2:
                            syncC.append(("x1o", (gi3 - 2) // 4,
                                          (gi3 - 2) % 4))
                            syncC.append(("h2o", (gi3 - 2) // 4,
                                          (gi3 - 2) % 4))
                for gi3 in (30, 31):
                    syncC.append(("x1o", gi3 // 4, gi3 % 4))
                    syncC.append(("h2o", gi3 // 4, gi3 % 4))
                idxC = {t: i for i, t in enumerate(syncC)}
                baseA3 = cnt["sA"]

                def wA_C(eng, key):
                    if key[0] == "v":
                        W(eng, "sVld%d" % (key[1] % 2),
                          16 * (key[1] // 2 + 1))
                    elif key[0] == "x":
                        gi3 = 4 * key[1] + key[2]
                        W(eng, "sX3%d" % (gi3 % 2), 16 * (gi3 // 2 + 1))
                    elif key[0] == "x1o":
                        gi3 = 4 * key[1] + key[2]
                        W(eng, "sX1o%d" % (gi3 % 2), 16 * (gi3 // 2 + 1))
                    else:
                        gi3 = 4 * key[1] + key[2]
                        W(eng, "sH2o%d" % (gi3 % 2), 16 * (gi3 // 2 + 1))

                @blockC.sync
                def _(sync):
                    if "C" not in DBG_PHASES:
                        return
                    for t in syncC:
                        if t[0] == "v":
                            e = t[1]
                            W(sync, "s3_rvc", 8 * (e - 1))
                            I(sync.dma_start(
                                vt3[:, e % 2, :, :],
                                vsp_d[:, :, e * TB:(e + 1) * TB]
                                .rearrange("c p t -> p c t")),
                              "sVld%d" % (e % 2), 16)
                        elif t[0] == "x":
                            e, c4 = t[1], t[2]
                            gi3 = 4 * e + c4
                            W(sync, "s3_x1", gi3 - 1)
                            I(sync.dma_start(
                                xb3[:, gi3 % 2, :],
                                x_d[gi3 * 128:(gi3 + 1) * 128, :]),
                              "sX3%d" % (gi3 % 2), 16)
                        elif t[0] == "x1o":
                            e, c4 = t[1], t[2]
                            gi3 = 4 * e + c4
                            W(sync, "s3_x1", gi3 + 1)
                            I(sync.dma_start(
                                x1_d[gi3 * 128:(gi3 + 1) * 128, :],
                                x1b[:, gi3 % 2, :]),
                              "sX1o%d" % (gi3 % 2), 16)
                        else:
                            e, c4 = t[1], t[2]
                            gi3 = 4 * e + c4
                            W(sync, "s3_h2", gi3 + 1)
                            I(sync.dma_start(
                                h2t_d[:, :, gi3 * 128:(gi3 + 1) * 128]
                                .rearrange("c p t -> p c t"),
                                h2tb[:, gi3 % 2, :, :]),
                              "sH2o%d" % (gi3 % 2), 16)
                    for s in ["sVld0", "sVld1", "sX30", "sX31", "sX1o0",
                              "sX1o1", "sH2o0", "sH2o1"]:
                        sync.wait_ge(sem(s), cnt[s])

                @blockC.gpsimd
                def _(gpsimd):
                    if "C" not in DBG_PHASES:
                        return
                    I(gpsimd.dma_start(
                        wo_s[:, :, :],
                        wo_d.rearrange("(ci p) f -> p ci f", p=128)),
                      "sG3", 16)
                    I(gpsimd.dma_start(selr[:, :], selr_d), "sG3", 16)
                    I(gpsimd.dma_start(
                        wit[:, :, :],
                        wit_d.rearrange("(ci p) f -> p ci f", p=128)),
                      "sG3", 16)
                    I(gpsimd.dma_start(bcat[:, :], bcat_d), "sG3", 16)
                    gpsimd.wait_ge(sem("sG3"), cnt["sG3"])

                @blockC.tensor
                def _(tensor):
                    if "C" not in DBG_PHASES:
                        return
                    if DBG_ZERO_R:
                        W(tensor, "s_dbgz", 2)
                    W(tensor, "sG3", 16 * 4)
                    for e in range(NB):
                        rsrc = r_ev if e % 2 == 0 else r_od
                        rcol = (e // 2) * 512
                        for cc in range(DC):
                            ii = 8 * e + cc
                            W(tensor, "s3_rvc", ii - 1)
                            mm = tensor.matmul(
                                ps_rb[:, cc % 2, :],
                                selr[0:16, cc * 128:(cc + 1) * 128],
                                rsrc[0:16, rcol:rcol + 512],
                                start=True, stop=True)
                            I(mm, "s3_rb")
                        W(tensor, "s3_rvc", 8 * (e + 1))
                        for co in range(DC):
                            ii = 8 * e + co
                            W(tensor, "s3_ev", ii - 1)
                            for ci in range(DC):
                                mm = tensor.matmul(
                                    ps_wo[:, co % 2, :],
                                    wo_s[:, ci, co * 128:(co + 1) * 128],
                                    rv[:, e % 2, ci, :],
                                    start=(ci == 0), stop=(ci == DC - 1))
                            I(mm, "s3_wo")
                        for c4 in range(4):
                            gi3 = 4 * e + c4
                            W(tensor, "s3_ev", 8 * (e + 1))
                            W(tensor, "s3_ae", gi3 - 1)
                            for j in range(DC):
                                mm = tensor.matmul(
                                    ps_at[:, gi3 % 2,
                                          j * 128:(j + 1) * 128],
                                    at_sb[:, e % 2, j,
                                          c4 * 128:(c4 + 1) * 128],
                                    identb[:, :], is_transpose=True,
                                    start=(j == 0), stop=(j == DC - 1))
                            I(mm, "s3_at")
                            W(tensor, "s3_h2a", gi3 + 1)
                            W(tensor, "s3_h2", gi3 - 1)
                            for j in range(DC):
                                mm = tensor.matmul(
                                    ps_h2[:, gi3 % 2,
                                          j * 128:(j + 1) * 128],
                                    h2b[:, j * 128:(j + 1) * 128],
                                    identb[:, :], is_transpose=True,
                                    start=(j == 0), stop=(j == DC - 1))
                            I(mm, "s3_ath2")

                @blockC.scalar
                def _(scalar):
                    if "C" not in DBG_PHASES:
                        return
                    for e in range(NB):
                        for c4 in range(4):
                            gi3 = 4 * e + c4
                            W(scalar, "s3_var", 2 * gi3 + 1)
                            a0 = cnt["s_ac"]
                            I(scalar.activation(s3lv[:, 0:1], s3vr[:, 0:1],
                                                AF.Ln, bias=epsb[:, 0:1]),
                              "s_ac")
                            scalar.wait_ge(sem("s_ac"), a0 + 1)
                            I(scalar.activation(s3rs2[:, 0:1], s3lv[:, 0:1],
                                                AF.Exp, scale=-0.5),
                              "s3_rstd")
                            W(scalar, "s3_var", 2 * gi3 + 2)
                            a0 = cnt["s_ac"]
                            I(scalar.activation(s3lv[:, 0:1], s3vr[:, 0:1],
                                                AF.Ln, bias=epsb[:, 0:1]),
                              "s_ac")
                            scalar.wait_ge(sem("s_ac"), a0 + 1)
                            I(scalar.activation(s3rs3[:, 0:1], s3lv[:, 0:1],
                                                AF.Exp, scale=-0.5),
                              "s3_rstd")

                @blockC.vector
                def _(vector):
                    if "C" not in DBG_PHASES:
                        return
                    if DBG_ZERO_R:
                        I(vector.memset(r_ev[:, :], 0.0), "s_dbgz")
                        I(vector.memset(r_od[:, :], 0.0), "s_dbgz")
                        vector.wait_ge(sem("s_dbgz"), cnt["s_dbgz"])
                    for e in range(NB):
                        for cc in range(DC):
                            ii = 8 * e + cc
                            W(vector, "s3_rb", ii + 1)
                            wA_C(vector, ("v", e))
                            W(vector, "s3_wo", 8 * (e - 1))
                            I(vector.scalar_tensor_tensor(
                                rv[:, e % 2, cc, :], ps_rb[:, cc % 2, :],
                                1.0, vt3[:, e % 2, cc, :],
                                ALU.add, ALU.mult), "s3_rvc")
                        for co in range(DC):
                            ii = 8 * e + co
                            W(vector, "s3_wo", ii + 1)
                            W(vector, "s3_at", 4 * (e - 1))
                            I(vector.tensor_scalar(
                                at_sb[:, e % 2, co, :], ps_wo[:, co % 2, :],
                                bo[:, co:co + 1], None, ALU.add), "s3_ev")
                        for c4 in range(4):
                            gi3 = 4 * e + c4
                            W(vector, "s3_at", gi3 + 1)
                            I(vector.tensor_copy(att[:, :],
                                                 ps_at[:, gi3 % 2, :]),
                              "s3_ae")
                            vector.wait_ge(sem("s3_ae"), gi3 + 1)
                            W(vector, "s3_ath2", gi3)
                            v0 = cnt["s_dv"]
                            I(vector.reduce_sum(s3xs[:, 0:1], att[:, :],
                                                axis=AX.X), "s_dv")
                            I(vector.scalar_tensor_tensor(
                                h2b[:, :], att[:, :], 1.0, att[:, :],
                                ALU.mult, ALU.mult,
                                accum_out=s3xq[:, 0:1]), "s_dv")
                            vector.wait_ge(sem("s_dv"), v0 + 2)
                            I(vector.tensor_scalar(
                                s3nm[:, 0:1], s3xs[:, 0:1], -1.0 / D, None,
                                ALU.mult), "s_dv")
                            vector.wait_ge(sem("s_dv"), v0 + 3)
                            I(vector.tensor_mul(s3n2[:, 0:1], s3nm[:, 0:1],
                                                s3nm[:, 0:1]), "s_dv")
                            vector.wait_ge(sem("s_dv"), v0 + 4)
                            I(vector.scalar_tensor_tensor(
                                s3vr[:, 0:1], s3xq[:, 0:1], 1.0 / D,
                                s3n2[:, 0:1], ALU.mult, ALU.subtract),
                              "s3_var")
                            I(vector.tensor_copy(s3nm2[:, 0:1],
                                                 s3nm[:, 0:1]), "s_dv")
                            W(vector, "s3_rstd", 2 * gi3 + 1)
                            vector.wait_ge(sem("s_dv"), cnt["s_dv"])
                            I(vector.tensor_scalar(
                                t1[:, :], att[:, :], s3nm2[:, 0:1],
                                s3rs2[:, 0:1], ALU.add, ALU.mult), "s_dv")
                            wA_C(vector, ("x", e, c4))
                            if gi3 >= 2:
                                wA_C(vector, ("x1o", (gi3 - 2) // 4,
                                              (gi3 - 2) % 4))
                            vector.wait_ge(sem("s_dv"), cnt["s_dv"])
                            I(vector.tensor_add(x1b[:, gi3 % 2, :],
                                                t1[:, :],
                                                xb3[:, gi3 % 2, :]),
                              "s3_x1")
                            xv1 = x1b[:, gi3 % 2, :]
                            v0 = cnt["s_dv"]
                            vector.wait_ge(sem("s3_x1"), gi3 + 1)
                            I(vector.reduce_sum(s3xs[:, 0:1], xv1,
                                                axis=AX.X), "s_dv")
                            I(vector.scalar_tensor_tensor(
                                att[:, :], xv1, 1.0, xv1, ALU.mult,
                                ALU.mult, accum_out=s3xq[:, 0:1]), "s_dv")
                            vector.wait_ge(sem("s_dv"), v0 + 2)
                            I(vector.tensor_scalar(
                                s3nm[:, 0:1], s3xs[:, 0:1], -1.0 / D, None,
                                ALU.mult), "s_dv")
                            vector.wait_ge(sem("s_dv"), v0 + 3)
                            I(vector.tensor_mul(s3n2[:, 0:1], s3nm[:, 0:1],
                                                s3nm[:, 0:1]), "s_dv")
                            vector.wait_ge(sem("s_dv"), v0 + 4)
                            I(vector.scalar_tensor_tensor(
                                s3vr[:, 0:1], s3xq[:, 0:1], 1.0 / D,
                                s3n2[:, 0:1], ALU.mult, ALU.subtract),
                              "s3_var")
                            I(vector.tensor_copy(s3nm3[:, 0:1],
                                                 s3nm[:, 0:1]), "s_dv")
                            W(vector, "s3_rstd", 2 * gi3 + 2)
                            vector.wait_ge(sem("s_dv"), cnt["s_dv"])
                            I(vector.tensor_scalar(
                                h2b[:, :], xv1, s3nm3[:, 0:1],
                                s3rs3[:, 0:1], ALU.add, ALU.mult), "s3_h2a")
                            W(vector, "s3_ath2", gi3 + 1)
                            if gi3 >= 2:
                                wA_C(vector, ("h2o", (gi3 - 2) // 4,
                                              (gi3 - 2) % 4))
                            I(vector.tensor_copy(
                                h2tb[:, gi3 % 2, :, :],
                                ps_h2[:, gi3 % 2, :]
                                .rearrange("p (c t) -> p c t", t=128)),
                              "s3_h2")

            if DBG_DUMP:
                with nc.Block() as blkdbg3:
                    @blkdbg3.gpsimd
                    def _(gpsimd):
                        if "C" in DBG_PHASES:
                            I(gpsimd.dma_start(x1dump_d, x1_d), "sG", 16)
                            I(gpsimd.dma_start(h2dump_d, h2t_d), "sG", 16)
                        gpsimd.wait_ge(sem("sG"), cnt["sG"])

            with ExitStack() as sc4:
                e4 = sc4.enter_context
                h2t4 = e4(nc.sbuf_tensor("h2t4", [128, 2, DC, TB], BF16))
                lg_sb = e4(nc.sbuf_tensor("lg_sb", [8, TB], F32))
                lgt = e4(nc.sbuf_tensor("lgt", [128, 32], F32))
                mxv = e4(nc.sbuf_tensor("mxv", [128, 4], F32))
                mask_sb = e4(nc.sbuf_tensor("mask_sb", [128, 32], BF16))
                mkT = e4(nc.sbuf_tensor("mkT", [8, TB], BF16))
                xr_sb = e4(nc.sbuf_tensor("xr_sb", [32, TB], BF16))
                mxr = e4(nc.sbuf_tensor("mxr", [32, TB], BF16))
                interT = e4(nc.sbuf_tensor("interT", [128, FC, TB], BF16))
                wotb = e4(nc.sbuf_tensor("wotb", [128, 2, FC, 128], BF16))
                ft_sb = e4(nc.sbuf_tensor("ft_sb", [128, 2, DC, TB], BF16))
                ffb = e4(nc.sbuf_tensor("ffb", [128, 1024], BF16))
                t4 = e4(nc.sbuf_tensor("t4", [128, 1024], F32))
                yb = e4(nc.sbuf_tensor("yb", [128, 2, 1024], F32))
                x1b4 = e4(nc.sbuf_tensor("x1b4", [128, 2, 1024], F32))
                s4xs = e4(nc.sbuf_tensor("s4xs", [128, 1], F32))
                s4xq = e4(nc.sbuf_tensor("s4xq", [128, 1], F32))
                s4nm = e4(nc.sbuf_tensor("s4nm", [128, 1], F32))
                s4n2 = e4(nc.sbuf_tensor("s4n2", [128, 1], F32))
                s4vr = e4(nc.sbuf_tensor("s4vr", [128, 1], F32))
                s4lv = e4(nc.sbuf_tensor("s4lv", [128, 1], F32))
                s4rs = e4(nc.sbuf_tensor("s4rs", [128, 1], F32))
                ps_g = e4(nc.psum_tensor("ps_g", [128, 512], F32))
                ps_lgt = e4(nc.psum_tensor("ps_lgt", [128, 512], F32))
                ps_mt = e4(nc.psum_tensor("ps_mt", [8, 1024], BF16))
                ps_m32 = e4(nc.psum_tensor("ps_m32", [32, 512], F32))
                ps_xr = e4(nc.psum_tensor("ps_xr", [32, 512], F32))
                ps_wi = e4(nc.psum_tensor("ps_wi", [128, 2, 512], F32))
                ps_ft = e4(nc.psum_tensor("ps_ft", [128, 1024], BF16))
                blockD = e4(nc.Block())

                syncD = []
                for e in range(NB):
                    syncD.append(("h2i", e))
                    for c4 in range(4):
                        gi4 = 4 * e + c4
                        syncD.append(("x1i", e, c4))
                        if gi4 >= 2:
                            syncD.append(("yo", (gi4 - 2) // 4,
                                          (gi4 - 2) % 4))
                for gi4 in (30, 31):
                    syncD.append(("yo", gi4 // 4, gi4 % 4))
                idxD = {t: i for i, t in enumerate(syncD)}
                baseA4 = cnt["sA"]

                def wA_D(eng, key):
                    if key[0] == "h2i":
                        W(eng, "sH2i%d" % (key[1] % 2),
                          16 * (key[1] // 2 + 1))
                    elif key[0] == "x1i":
                        gi4 = 4 * key[1] + key[2]
                        W(eng, "sX1i%d" % (gi4 % 2), 16 * (gi4 // 2 + 1))
                    else:
                        gi4 = 4 * key[1] + key[2]
                        W(eng, "sYo%d" % (gi4 % 2), 16 * (gi4 // 2 + 1))

                @blockD.sync
                def _(sync):
                    if "D" not in DBG_PHASES:
                        return
                    for t in syncD:
                        if t[0] == "h2i":
                            e = t[1]
                            W(sync, "s4_womm", DC * (e - 1))
                            I(sync.dma_start(
                                h2t4[:, e % 2, :, :],
                                h2t_d[:, :, e * TB:(e + 1) * TB]
                                .rearrange("c p t -> p c t")),
                              "sH2i%d" % (e % 2), 16)
                        elif t[0] == "x1i":
                            e, c4 = t[1], t[2]
                            gi4 = 4 * e + c4
                            W(sync, "s4_y", gi4 - 1)
                            I(sync.dma_start(
                                x1b4[:, gi4 % 2, :],
                                x1_d[gi4 * 128:(gi4 + 1) * 128, :]),
                              "sX1i%d" % (gi4 % 2), 16)
                        else:
                            e, c4 = t[1], t[2]
                            gi4 = 4 * e + c4
                            W(sync, "s4_y", gi4 + 1)
                            I(sync.dma_start(
                                y_d[gi4 * 128:(gi4 + 1) * 128, :],
                                yb[:, gi4 % 2, :]),
                              "sYo%d" % (gi4 % 2), 16)
                    for s in ["sH2i0", "sH2i1", "sX1i0", "sX1i1",
                              "sYo0", "sYo1"]:
                        sync.wait_ge(sem(s), cnt[s])

                baseG_D = [0]

                @blockD.gpsimd
                def _(gpsimd):
                    if "D" not in DBG_PHASES:
                        return
                    for e in range(NB):
                        for co in range(DC):
                            oi = DC * e + co
                            W(gpsimd, "s4_womm", oi - 1)
                            I(gpsimd.dma_start(
                                wotb[:, co % 2, :, :],
                                wot_d.rearrange("(ci p) f -> p ci f",
                                                p=128)[:, :,
                                                       co * 128:
                                                       (co + 1) * 128]),
                              "sWot%d" % (oi % 2), 16)
                    gpsimd.wait_ge(sem("sWot0"), cnt["sWot0"])
                    gpsimd.wait_ge(sem("sWot1"), cnt["sWot1"])

                @blockD.tensor
                def _(tensor):
                    if "D" not in DBG_PHASES:
                        return
                    W(tensor, "sG3", 16 * 4)
                    for e in range(NB):
                        par = e % 2
                        wA_D(tensor, ("h2i", e))
                        W(tensor, "s4_lg", e)
                        for ci in range(DC):
                            mm = tensor.matmul(
                                ps_g[0:8, :], gwt[:, ci, :],
                                h2t4[:, par, ci, :],
                                start=(ci == 0), stop=(ci == DC - 1))
                        I(mm, "s4_gmm")
                        W(tensor, "s4_lg", e + 1)
                        W(tensor, "s4_lgt", e)
                        for c4 in range(4):
                            mm = tensor.matmul(
                                ps_lgt[:, c4 * 8:(c4 + 1) * 8],
                                lg_sb[0:8, c4 * 128:(c4 + 1) * 128],
                                identf8[:, :], is_transpose=True,
                                start=(c4 == 0), stop=(c4 == 3))
                        I(mm, "s4_lgtmm")
                        W(tensor, "s4_mkt", e)
                        for c4 in range(4):
                            W(tensor, "s4_mk", 4 * e + c4 + 1)
                            mm = tensor.matmul(
                                ps_mt[0:8, c4 * 128:(c4 + 1) * 128],
                                mask_sb[:, c4 * 8:(c4 + 1) * 8],
                                identb[:, :], is_transpose=True,
                                start=(c4 == 0), stop=(c4 == 3))
                        I(mm, "s4_mtr")
                        W(tensor, "s4_mkt", e + 1)
                        W(tensor, "s4_mx", e)
                        mm = tensor.matmul(ps_m32[:, :], selm[:, :],
                                           mkT[:, :], start=True, stop=True)
                        I(mm, "s4_m32mm")
                        W(tensor, "s4_xre", e)
                        for ci in range(DC):
                            mm = tensor.matmul(
                                ps_xr[:, :], acatt[:, ci, :],
                                h2t4[:, par, ci, :],
                                start=(ci == 0), stop=(ci == DC - 1))
                        I(mm, "s4_xrmm")
                        for fc in range(FC):
                            fi = FC * e + fc
                            W(tensor, "s4_ie", fi - 1)
                            if fc < 2:
                                W(tensor, "s4_fe", DC * e)
                                if fc == 0:
                                    W(tensor, "s4_mx", e + 1)
                            for ci in range(DC):
                                tensor.matmul(
                                    ps_wi[:, fc % 2, :],
                                    wit[:, ci, fc * 128:(fc + 1) * 128],
                                    h2t4[:, par, ci, :],
                                    start=(ci == 0), stop=False)
                            mm = tensor.matmul(
                                ps_wi[:, fc % 2, :],
                                bcat[:, fc * 128:(fc + 1) * 128],
                                mxr[:, :], start=False, stop=True)
                            I(mm, "s4_wimm")
                        W(tensor, "s4_ie", FC * (e + 1))
                        for co in range(DC):
                            oi = DC * e + co
                            W(tensor, "s4_fe", oi - 1)
                            W(tensor, "sWot%d" % (oi % 2),
                              16 * (oi // 2 + 1))
                            for fc in range(FC):
                                mm = tensor.matmul(
                                    ps_wi[:, co % 2, :],
                                    wotb[:, co % 2, fc, :],
                                    interT[:, fc, :],
                                    start=(fc == 0), stop=(fc == FC - 1))
                            I(mm, "s4_womm")
                        W(tensor, "s4_fe", DC * (e + 1))
                        for c4 in range(4):
                            gi4 = 4 * e + c4
                            W(tensor, "s4_ffe", gi4)
                            for j in range(DC):
                                mm = tensor.matmul(
                                    ps_ft[:, j * 128:(j + 1) * 128],
                                    ft_sb[:, par, j,
                                          c4 * 128:(c4 + 1) * 128],
                                    identb[:, :], is_transpose=True,
                                    start=(j == 0), stop=(j == DC - 1))
                            I(mm, "s4_ftr")

                @blockD.scalar
                def _(scalar):
                    if "D" not in DBG_PHASES:
                        return
                    for e in range(NB):
                        for c4 in range(4):
                            gi4 = 4 * e + c4
                            W(scalar, "s4_var", gi4 + 1)
                            a0 = cnt["s_ac"]
                            I(scalar.activation(s4lv[:, 0:1], s4vr[:, 0:1],
                                                AF.Ln, bias=epsb[:, 0:1]),
                              "s_ac")
                            scalar.wait_ge(sem("s_ac"), a0 + 1)
                            I(scalar.activation(s4rs[:, 0:1], s4lv[:, 0:1],
                                                AF.Exp, scale=-0.5),
                              "s4_rstd")

                @blockD.vector
                def _(vector):
                    if "D" not in DBG_PHASES:
                        return
                    for e in range(NB):
                        W(vector, "s4_gmm", e + 1)
                        W(vector, "s4_lgtmm", e)
                        I(vector.tensor_scalar(lg_sb[:, :], ps_g[0:8, :],
                                               gb[0:8, 0:1], None, ALU.add),
                          "s4_lg")
                        W(vector, "s4_lgtmm", e + 1)
                        I(vector.tensor_copy(lgt[:, :], ps_lgt[:, 0:32]),
                          "s4_lgt")
                        vector.wait_ge(sem("s4_lgt"), e + 1)
                        v0 = cnt["s_dv"]
                        for c4 in range(4):
                            I(vector.reduce_max(
                                mxv[:, c4:c4 + 1],
                                lgt[:, c4 * 8:(c4 + 1) * 8], axis=AX.X),
                              "s_dv")
                        vector.wait_ge(sem("s_dv"), v0 + 4)
                        for c4 in range(4):
                            I(vector.tensor_single_scalar(
                                mask_sb[:, c4 * 8:(c4 + 1) * 8],
                                lgt[:, c4 * 8:(c4 + 1) * 8],
                                mxv[:, c4:c4 + 1], ALU.is_equal), "s4_mk")
                        W(vector, "s4_mtr", e + 1)
                        W(vector, "s4_m32mm", e)
                        I(vector.tensor_copy(mkT[:, :], ps_mt[0:8, 0:512]),
                          "s4_mkt")
                        W(vector, "s4_xrmm", e + 1)
                        I(vector.tensor_copy(xr_sb[:, :], ps_xr[0:32, :]),
                          "s4_xre")
                        W(vector, "s4_m32mm", e + 1)
                        W(vector, "s4_wimm", FC * e)
                        vector.wait_ge(sem("s4_xre"), e + 1)
                        I(vector.tensor_tensor(mxr[:, :], xr_sb[:, :],
                                               ps_m32[0:32, :], ALU.mult),
                          "s4_mx")
                        if e >= 1:
                            W(vector, "s4_womm", DC * e)
                        for fc in range(FC):
                            fi = FC * e + fc
                            W(vector, "s4_wimm", fi + 1)
                            I(vector.tensor_scalar(
                                interT[:, fc, :], ps_wi[:, fc % 2, :],
                                wib[:, fc:fc + 1], 0.0, ALU.add, ALU.max),
                              "s4_ie")
                        for co in range(DC):
                            oi = DC * e + co
                            W(vector, "s4_womm", oi + 1)
                            W(vector, "s4_ftr", 4 * (e - 1))
                            I(vector.tensor_scalar(
                                ft_sb[:, e % 2, co, :], ps_wi[:, co % 2, :],
                                wob[:, co:co + 1], None, ALU.add), "s4_fe")
                        for c4 in range(4):
                            gi4 = 4 * e + c4
                            W(vector, "s4_ftr", gi4 + 1)
                            I(vector.tensor_copy(ffb[:, :], ps_ft[:, :]),
                              "s4_ffe")
                            vector.wait_ge(sem("s4_ffe"), gi4 + 1)
                            v0 = cnt["s_dv"]
                            I(vector.reduce_sum(s4xs[:, 0:1], ffb[:, :],
                                                axis=AX.X), "s_dv")
                            I(vector.scalar_tensor_tensor(
                                t4[:, :], ffb[:, :], 1.0, ffb[:, :],
                                ALU.mult, ALU.mult,
                                accum_out=s4xq[:, 0:1]), "s_dv")
                            vector.wait_ge(sem("s_dv"), v0 + 2)
                            I(vector.tensor_scalar(
                                s4nm[:, 0:1], s4xs[:, 0:1], -1.0 / D, None,
                                ALU.mult), "s_dv")
                            vector.wait_ge(sem("s_dv"), v0 + 3)
                            I(vector.tensor_mul(s4n2[:, 0:1], s4nm[:, 0:1],
                                                s4nm[:, 0:1]), "s_dv")
                            vector.wait_ge(sem("s_dv"), v0 + 4)
                            I(vector.scalar_tensor_tensor(
                                s4vr[:, 0:1], s4xq[:, 0:1], 1.0 / D,
                                s4n2[:, 0:1], ALU.mult, ALU.subtract),
                              "s4_var")
                            W(vector, "s4_rstd", gi4 + 1)
                            vector.wait_ge(sem("s_dv"), cnt["s_dv"])
                            I(vector.tensor_scalar(
                                t4[:, :], ffb[:, :], s4nm[:, 0:1],
                                s4rs[:, 0:1], ALU.add, ALU.mult), "s_dv")
                            wA_D(vector, ("x1i", e, c4))
                            if gi4 >= 2:
                                wA_D(vector, ("yo", (gi4 - 2) // 4,
                                              (gi4 - 2) % 4))
                            vector.wait_ge(sem("s_dv"), cnt["s_dv"])
                            I(vector.tensor_add(yb[:, gi4 % 2, :],
                                                t4[:, :],
                                                x1b4[:, gi4 % 2, :]),
                              "s4_y")

    return nc


_PROGRAM_CACHE = {}


def kernel(**inputs):
    X = np.ascontiguousarray(inputs["L0_attn_X_embed"], dtype=np.float32)
    Wq = np.ascontiguousarray(inputs["L0_attn_Wq"], dtype=np.float32)
    Wk = np.ascontiguousarray(inputs["L0_attn_Wk"], dtype=np.float32)
    Wv = np.ascontiguousarray(inputs["L0_attn_Wv"], dtype=np.float32)
    Wo = np.ascontiguousarray(inputs["L0_attn_Wo"], dtype=np.float32)

    def b128(v, cols):
        return np.ascontiguousarray(
            np.asarray(v, dtype=np.float32).reshape(cols, 128).T)

    bq = b128(inputs["L0_attn_bq"], 8)
    bk = b128(inputs["L0_attn_bk"], 8)
    bv = b128(inputs["L0_attn_bv"], 8)
    bo = b128(inputs["L0_attn_bo"], 8)
    wit = np.ascontiguousarray(
        np.asarray(inputs["L0_ffn_wi_W"], dtype=np.float32).T)
    wib = b128(inputs["L0_ffn_wi_b"], 32)
    wot = np.ascontiguousarray(
        np.asarray(inputs["L0_ffn_wo_W"], dtype=np.float32).T)
    wob = b128(inputs["L0_ffn_wo_b"], 8)
    gwt = np.ascontiguousarray(
        np.asarray(inputs["L0_ffn_gate_W"], dtype=np.float32).T)
    gbv = np.ascontiguousarray(
        np.asarray(inputs["L0_ffn_gate_b"], dtype=np.float32).reshape(8, 1))
    acatt = np.ascontiguousarray(
        np.asarray(inputs["L0_ffn_lora_A"], dtype=np.float32)
        .reshape(32, D).T)
    bcat = np.ascontiguousarray(
        np.asarray(inputs["L0_ffn_lora_B"], dtype=np.float32)
        .transpose(0, 2, 1).reshape(32, FF))
    ident = np.eye(128, dtype=np.float32)
    ones16 = np.ones((128, 16), dtype=np.float32)
    selr = np.zeros((16, 1024), dtype=np.float32)
    for cc in range(8):
        for m in range(128):
            selr[2 * cc + m // 64, cc * 128 + m] = 1.0
    epsc = np.stack([np.full(128, EPS, dtype=np.float32),
                     np.full(128, -0.75 * np.log(2.0),
                             dtype=np.float32)], axis=1)
    selm = np.zeros((8, 32), dtype=np.float32)
    for i in range(8):
        selm[i, 4 * i:4 * i + 4] = 1.0

    if "nc" not in _PROGRAM_CACHE:
        _PROGRAM_CACHE["nc"] = build_program()
    nc = _PROGRAM_CACHE["nc"]
    common = dict(wq=Wq, wk=Wk, wv=Wv, wo=Wo, bq=bq, bk=bk, bv=bv, bo=bo,
                  wit=wit, wib=wib, wot=wot, wob=wob, gwt=gwt, gb=gbv,
                  acatt=acatt, bcat=bcat, ident=ident, ones16=ones16,
                  selr=selr, selm=selm, epsc=epsc)
    in_maps = []
    for c in range(8):
        n, h = c // 2, c % 2
        xs = np.ascontiguousarray(X[n, h * TOK:(h + 1) * TOK, :])
        in_maps.append(dict(common, x=xs))
    res = run_bass_kernel_spmd(nc, in_maps, core_ids=list(range(8)))
    out = np.empty((4, 8192, D), dtype=np.float32)
    for c in range(8):
        n, h = c // 2, c % 2
        out[n, h * TOK:(h + 1) * TOK, :] = np.asarray(res.results[c]["y"])
    return out
